# version 07
# baseline (speedup 1.0000x reference)
import math
import numpy as np
import ml_dtypes
import concourse.bass as bass
import concourse.mybir as mybir
from concourse.bass_isa import ReduceOp
from concourse.bass_utils import run_bass_kernel_spmd

F32 = mybir.dt.float32
BF16 = mybir.dt.bfloat16
I32 = mybir.dt.int32
AF = mybir.ActivationFunctionType
ALU = mybir.AluOpType
AX = mybir.AxisListType

P = 128
T = 2048
D = 1024
NCH = 8
FF = 4096
NE = 8
CAP_K = 24576
SOFT_SHIFT = 20.0
L1_SHIFT = 12.0
L1_NB = 32
L1_MD = 128
L1_I = 3072
NEG_BIG = -1e30
RG = [[0, 1, 2, 3, 4, 5, 6, 7]]

_CONFIG = {"phases": 4}

CONST_VALS = [0.0, 1.0, -SOFT_SHIFT, -L1_SHIFT, 1e-5, -0.5, 0.5, -8.0,
              1.0 / 1024.0, 0.125, -32768.0, 1.0 / 65536.0, 65535.0, 32767.0,
              32768.5, 8.5, float(CAP_K), 2.0, -1.0, -2.0, math.sqrt(P),
              1.0 / 7.0, 7.0, 65536.0, 1.0 / 32767.0, -65536.0, 8.0]


def _bucket_of(n):
    me = L1_NB // 2
    if n < me:
        return n
    v = me + math.log(max(n, 1) / me) / math.log(L1_MD / me) * (L1_NB - me)
    return int(min(v, L1_NB - 1))


class Prog:
    ENGS = ("sync", "tensor", "vector", "scalar", "gpsimd")

    def __init__(self):
        self.ops = {e: [] for e in self.ENGS}
        self.cnt = {"T": 0, "V": 0, "S": 0, "G": 0, "W": 0, "B": 0, "Q": 0}
        self.sem = {}
        self.dma_semnames = {"W", "B", "Q"}
        self.scope = None

    def emit(self, eng, fn):
        self.ops[eng].append((self.scope, fn))

    def wait(self, eng, sem_name, val):
        if val is None or val <= 0:
            return
        s = self.sem[sem_name]
        mult = 16 if sem_name in self.dma_semnames else 1
        self.emit(eng, lambda e, s=s, v=mult * val: e.wait_ge(s, v))

    AUTO = {"vector": "V", "scalar": "S", "gpsimd": "G"}

    def run(self, eng, builder, inc=None):
        is_dma = inc in self.dma_semnames
        auto = self.AUTO.get(eng)
        if auto is not None and not is_dma:
            prev = self.cnt[auto]
            sa = self.sem[auto]
            self.emit(eng, lambda e, b=builder, sa=sa, prev=prev: (
                e.wait_ge(sa, prev) if prev > 0 else None,
                b(e).then_inc(sa, 1)))
            self.cnt[auto] += 1
            return self.cnt[auto]
        if inc is None:
            self.emit(eng, lambda e, b=builder: b(e))
            return None
        s = self.sem[inc]
        mult = 16 if is_dma else 1
        self.emit(eng, lambda e, b=builder, s=s, m=mult: b(e).then_inc(s, m))
        self.cnt[inc] += 1
        return self.cnt[inc]

    def add_dma_sem(self, name, handle):
        self.sem[name] = handle
        self.cnt[name] = 0
        self.dma_semnames.add(name)

    def snap(self, name):
        return self.cnt[name]

    def finalize(self, nc):
        with nc.Block() as block:
            for eng in self.ENGS:
                lst = self.ops[eng]
                if not lst:
                    continue
                def runner(e, lst=lst):
                    for scope, fn in lst:
                        if scope is None:
                            fn(e)
                        else:
                            with nc.named_scope(scope):
                                fn(e)
                getattr(block, eng)(runner)


def build_nc(phases=4):
    nc = bass.Bass("TRN2", target_bir_lowering=False)

    def din(name, shape, dtype=F32):
        return nc.dram_tensor(name, list(shape), dtype, kind="ExternalInput")

    xT = din("xT", (D, T))
    wqkv = din("wqkv", (D, 3 * D))
    bqkv_r = din("bqkv_r", (P, 24))
    wout = din("wout", (D, D))
    bout_r = din("bout_r", (P, 8))
    gwT = din("gwT", (D, NE))
    w1T = din("w1T", (NE, D, FF))
    w2T = din("w2T", (NE, FF, D))
    wqk = din("wqk", (D, P))
    wv = din("wv", (D, D))
    gammaT = din("gammaT", (P, 2))
    rel_in = din("rel_in", (L1_NB, 1))
    wup_n = din("wup_n", (L1_I, D))
    wgate_n = din("wgate_n", (L1_I, D))
    wdown_n = din("wdown_n", (D, L1_I))
    ident_c = din("ident_c", (P, P), BF16)
    diagm_c = din("diagm_c", (P, P), BF16)
    ishift_c = din("ishift_c", (64, P), BF16)
    oh_c = din("oh_c", (L1_NB, 1280))
    maskadd_c = din("maskadd_c", (1, 1280))
    consts_c = din("consts_c", (32,))

    out_t = nc.dram_tensor("out", [T, D], F32, kind="ExternalOutput")
    dbg_t = nc.dram_tensor("dbg", [16, P, D], F32, kind="ExternalOutput") \
        if (_CONFIG.get("dbg3") or _CONFIG.get("dbg4")) else None
    g_hbm = nc.dram_tensor("g_hbm", [NE, T], F32, kind="Internal")
    sext_hbm = nc.dram_tensor("sext_hbm", [1280], F32, kind="Internal")
    ccg_in = nc.dram_tensor("ccg_in", [NE * T], F32, kind="Internal")
    ccg_out = nc.dram_tensor("ccg_out", [8 * NE * T], F32, kind="Internal")
    ccm_in = [nc.dram_tensor(f"ccm{i}_in", [16], F32, kind="Internal") for i in range(4)]
    ccm_out = [nc.dram_tensor(f"ccm{i}_out", [16], F32, kind="Internal") for i in range(4)]
    bis_hbm = nc.dram_tensor("bis_hbm", [16], F32, kind="Internal")
    up_hbm = nc.dram_tensor("up_hbm", [L1_I, T], BF16, kind="Internal")
    gate_hbm = nc.dram_tensor("gate_hbm", [L1_I, T], BF16, kind="Internal")

    cms = []

    def enter(cm):
        cms.append(cm)
        return cm.__enter__()

    RX = enter(nc.sbuf_tensor("RX", [P, NCH * T], F32))
    XLNt = enter(nc.sbuf_tensor("XLN", [P, NCH * T], BF16))
    XLN = XLNt[:, :]
    SCRA = enter(nc.sbuf_tensor("SCRA", [P, T], F32))
    SCRB = enter(nc.sbuf_tensor("SCRB", [P, T], F32))
    SQBF = enter(nc.sbuf_tensor("SQBF", [P, T], F32))
    SCRC = enter(nc.sbuf_tensor("SCRC", [P, T], F32))
    CB = enter(nc.sbuf_tensor("CB", [P, 32], F32))
    ONESB = enter(nc.sbuf_tensor("ONESB", [P, 16], BF16))
    ONES1 = enter(nc.sbuf_tensor("ONES1", [1, P], F32))
    IDENT = enter(nc.sbuf_tensor("IDENT", [P, P], BF16))
    IDENTF = enter(nc.sbuf_tensor("IDENTF", [P, P], F32))
    DIAGM = enter(nc.sbuf_tensor("DIAGM", [P, P], BF16))
    ISH = enter(nc.sbuf_tensor("ISH", [64, P], BF16))
    BQKV = enter(nc.sbuf_tensor("BQKV", [P, 24], F32))
    BOUT = enter(nc.sbuf_tensor("BOUT", [P, 8], F32))
    GAMT = enter(nc.sbuf_tensor("GAMT", [P, 8], F32))
    LROW = enter(nc.sbuf_tensor("LROW", [L1_NB, 4], F32))
    MXW = enter(nc.sbuf_tensor("MXW", [P, 32], F32))
    BIS = enter(nc.sbuf_tensor("BIS", [P, 16], F32))
    DQS = enter(nc.sbuf_tensor("DQS", [P, 96], F32))
    BIGt = enter(nc.sbuf_tensor("BIG", [P, 39296], BF16))
    BIG = BIGt[:, :]

    def big(o, cols):
        assert o + cols <= 39296, (o, cols)
        return BIG[:, o:o + cols]

    def bigf(o, n):
        return BIG[:, o:o + 2 * n].bitcast(F32)

    PB = [enter(nc.psum_tensor(f"pb{i}", [P, 512], F32)) for i in range(6)]
    PXB = enter(nc.psum_tensor("pxb", [P, 512], F32))
    PT = enter(nc.psum_tensor("pt", [P, 1024], BF16))

    p = Prog()
    for s in ("T", "V", "S", "G", "W", "B", "Q"):
        p.sem[s] = enter(nc.semaphore(s))
    RINGSEMS = ["R%d" % i for i in range(8)] + ["q0", "q1", "wv0", "wv1", "na0", "na1",
                "st0", "st1", "rr0", "rr1", "gb0", "cc0", "dbx"]
    for s in RINGSEMS:
        p.add_dma_sem(s, enter(nc.semaphore(s)))

    for i, v in enumerate(CONST_VALS):
        nc.const_aps.register(F32, float(v), CB[:, i:i + 1])

    for args in [
        lambda e: e.dma_start(CB[:, 0:32], bass.AP(consts_c, 0, [[0, P], [1, 32]])),
        lambda e: e.dma_start(IDENT[:, :], ident_c.ap()),
        lambda e: e.dma_start(DIAGM[:, :], diagm_c.ap()),
        lambda e: e.dma_start(ISH[:, :], ishift_c.ap()),
        lambda e: e.dma_start(BQKV[:, :], bqkv_r.ap()),
        lambda e: e.dma_start(BOUT[:, :], bout_r.ap()),
        lambda e: e.dma_start(GAMT[:, 0:2], gammaT.ap()),
        lambda e: e.dma_start(LROW[:, 0:1], rel_in.ap()),
    ]:
        p.run("scalar", args, inc="Q")
    q_pre = p.snap("Q")
    for c in range(NCH):
        p.run("sync", lambda e, c=c: e.dma_start(RX[:, c * T:(c + 1) * T], xT.ap()[c * P:(c + 1) * P, :]), inc="B")
    b_x = p.snap("B")
    p.wait("gpsimd", "Q", q_pre)
    p.wait("scalar", "Q", q_pre)
    p.wait("vector", "Q", q_pre)
    p.wait("tensor", "Q", q_pre)
    p.run("gpsimd", lambda e: e.memset(ONESB[:, :], 1.0))
    p.run("gpsimd", lambda e: e.tensor_copy(IDENTF[:, :], IDENT[:, :]))
    p.run("gpsimd", lambda e: e.memset(ONES1[:, :], 1.0))
    p.run("gpsimd", lambda e: e.memset(MXW[:, :], 0.0))
    p.run("gpsimd", lambda e: e.memset(BIS[:, :], 0.0), inc="G")
    g_ones = p.snap("G")

    def layer_norm(pre_v):
        p.wait("vector", "V", pre_v)
        for c in range(NCH):
            p.run("vector", lambda e, c=c: e.tensor_copy(XLN[:, c * T:(c + 1) * T], RX[:, c * T:(c + 1) * T]),
                  inc="V" if c == NCH - 1 else None)
        v_cast = p.snap("V")
        v_norm_prev = [0]
        SQB = SQBF[:, :].bitcast(BF16)
        for tc in range(4):
            tsl = slice(tc * 512, (tc + 1) * 512)
            p.wait("scalar", "V", v_cast)
            for c in range(NCH):
                p.run("scalar", lambda e, c=c, tsl=tsl: e.activation(
                    SQB[:, c * 512:(c + 1) * 512], RX[:, c * T:(c + 1) * T][:, tsl], AF.Square),
                    inc="S" if c == NCH - 1 else None)
            s_sq = p.snap("S")
            p.wait("tensor", "V", v_cast)
            p.wait("tensor", "S", s_sq)
            p.wait("tensor", "G", g_ones)
            for c in range(NCH):
                p.run("tensor", lambda e, c=c, tsl=tsl: e.matmul(
                    PB[4][0:1, :], ONESB[:, 0:1], XLN[:, c * T:(c + 1) * T][:, tsl],
                    start=(c == 0), stop=(c == NCH - 1)))
            for c in range(NCH):
                p.run("tensor", lambda e, c=c: e.matmul(
                    PB[5][0:1, :], ONESB[:, 0:1], SQB[:, c * 512:(c + 1) * 512],
                    start=(c == 0), stop=(c == NCH - 1)),
                    inc="T" if c == NCH - 1 else None)
            t_sums = p.snap("T")
            p.wait("scalar", "T", t_sums)
            p.wait("scalar", "V", v_norm_prev[0])
            p.run("scalar", lambda e: e.activation(SCRB[0:1, 0:512], PB[4][0:1, :], AF.Copy))
            p.run("scalar", lambda e: e.activation(SCRB[0:1, 512:1024], PB[5][0:1, :], AF.Copy), inc="S")
            s_rows = p.snap("S")
            p.wait("tensor", "S", s_rows)
            p.run("tensor", lambda e: e.matmul(PB[2][:, :], ONES1[0:1, :], SCRB[0:1, 0:512], start=True, stop=True))
            p.run("tensor", lambda e: e.matmul(PB[3][:, :], ONES1[0:1, :], SCRB[0:1, 512:1024], start=True, stop=True), inc="T")
            t_bc = p.snap("T")
            p.wait("vector", "T", t_bc)
            p.run("vector", lambda e: e.tensor_scalar(SCRB[:, 0:512], PB[2][:, :], 1.0 / 1024.0, None, ALU.mult))
            p.run("vector", lambda e: e.tensor_mul(SCRB[:, 512:1024], SCRB[:, 0:512], SCRB[:, 0:512]))
            p.run("vector", lambda e: e.scalar_tensor_tensor(
                SCRB[:, 1024:1536], PB[3][:, :], 1.0 / 1024.0, SCRB[:, 512:1024], ALU.mult, ALU.subtract), inc="V")
            v_var = p.snap("V")
            p.wait("scalar", "V", v_var)
            p.run("scalar", lambda e: e.activation(SCRB[:, 1536:2048], SCRB[:, 1024:1536], AF.Ln, bias=1e-5))
            p.run("scalar", lambda e: e.activation(SCRB[:, 1536:2048], SCRB[:, 1536:2048], AF.Exp, scale=-0.5), inc="S")
            s_rstd = p.snap("S")
            p.wait("vector", "S", s_rstd)
            for c in range(NCH):
                p.run("vector", lambda e, c=c, tsl=tsl: e.tensor_sub(
                    SCRA[:, 0:512], RX[:, c * T:(c + 1) * T][:, tsl], SCRB[:, 0:512]))
                p.run("vector", lambda e, c=c, tsl=tsl: e.tensor_mul(
                    XLN[:, c * T:(c + 1) * T][:, tsl], SCRA[:, 0:512], SCRB[:, 1536:2048]),
                    inc="V" if c == NCH - 1 else None)
            v_norm_prev[0] = p.snap("V")
        return p.snap("V")

    state = {"pt_free_v": 0, "last_exp_s": 0, "proj_evac_v": 0, "pxb_free_v": 0, "linv_used_v": 0}

    O_Q, O_K, O_VT, O_EXPS, O_CTX, O_WQT, O_WOT, O_OST = \
        0, 4096, 8192, 12512, 14560, 30944, 31968, 32992

    def phase1():
        p.wait("vector", "B", b_x)
        v_ln = layer_norm(0)
        p.wait("gpsimd", "G", g_ones)
        for pr_i in range(2):
            base = O_VT + pr_i * 2080
            p.run("gpsimd", lambda e, base=base: e.memset(
                bass.AP(BIGt, base + 64, [[39296, P], [130, 16]]), 1.0))
            p.run("gpsimd", lambda e, base=base: e.memset(
                bass.AP(BIGt, base + 129, [[39296, P], [130, 16]]), 1.0),
                inc="G" if pr_i == 1 else None)
        g_vt = p.snap("G")

        tile_plan = []
        for g in range(4):
            for part in range(3):
                for tcp in range(2):
                    for pr_i in range(2):
                        for kc in range(NCH):
                            tile_plan.append((kc, part * D + (2 * g + pr_i) * P))
        wq_marks = []
        wq_consumed = []

        def issue_wq(n):
            kc, col0 = tile_plan[n]
            if n >= NCH:
                p.wait("gpsimd", "T", wq_consumed[n - NCH])
            sname = "R%d" % (n % NCH)
            p.run("gpsimd", lambda e, kc=kc, col0=col0, n=n: e.dma_start(
                big(O_WQT + (n % NCH) * P, P), wqkv.ap()[kc * P:(kc + 1) * P, col0:col0 + P]), inc=sname)
            wq_marks.append((sname, p.snap(sname)))

        tile_i = [0]

        def proj_piece(g, part, tcp):
            p.wait("tensor", "V", v_ln)
            p.wait("tensor", "S", state["last_exp_s"])
            for pr_i in range(2):
                pr = 2 * g + pr_i
                p.wait("tensor", "V", state["proj_evac_v"])
                for kc in range(NCH):
                    n = tile_i[0]
                    while len(wq_marks) <= min(n + NCH - 1, len(tile_plan) - 1):
                        issue_wq(len(wq_marks))
                    p.wait("tensor", wq_marks[n][0], wq_marks[n][1])
                    for sub in range(2):
                        tc = 2 * tcp + sub
                        tsl = slice(tc * 512, (tc + 1) * 512)
                        p.run("tensor", lambda e, n=n, kc=kc, tsl=tsl, sub=sub: e.matmul(
                            PB[sub][:, :], big(O_WQT + (n % NCH) * P, P),
                            XLN[:, kc * T:(kc + 1) * T][:, tsl],
                            start=(kc == 0), stop=(kc == NCH - 1)),
                            inc="T" if sub == 1 else None)
                    wq_consumed.append(p.cnt["T"])
                    tile_i[0] += 1
                t_mm = p.cnt["T"]
                for sub in range(2):
                    tc = 2 * tcp + sub
                    tsl = slice(tc * 512, (tc + 1) * 512)
                    p.wait("vector", "T", t_mm)
                    if part == 0:
                        dst = big(O_Q + pr_i * T, T)[:, tsl]
                        p.run("vector", lambda e, dst=dst, sub=sub, pr=pr: e.tensor_scalar(
                            dst, PB[sub][:, :], BQKV[:, pr:pr + 1], None, ALU.add), inc="V")
                    elif part == 1:
                        dst = big(O_K + pr_i * T, T)[:, tsl]
                        p.run("vector", lambda e, dst=dst, sub=sub, pr=pr: e.tensor_scalar(
                            dst, PB[sub][:, :], BQKV[:, 8 + pr:9 + pr], 0.125, ALU.add, ALU.mult), inc="V")
                    else:
                        vbf = SQBF[:, :].bitcast(BF16)
                        p.run("vector", lambda e, sub=sub, pr=pr, tc=tc, vbf=vbf: e.tensor_scalar(
                            vbf[:, (tc % 2) * 512:(tc % 2) * 512 + 512], PB[sub][:, :],
                            BQKV[:, 16 + pr:17 + pr], None, ALU.add), inc="V")
                        v_vb = p.snap("V")
                        p.wait("tensor", "V", v_vb)
                        p.wait("tensor", "G", g_vt)
                        for j in range(4):
                            kt = tc * 4 + j
                            p.wait("tensor", "V", state["pt_free_v"])
                            p.run("tensor", lambda e, tc=tc, j=j, vbf=vbf: e.transpose(
                                PT[:, 0:128], vbf[:, (tc % 2) * 512 + j * 128:(tc % 2) * 512 + (j + 1) * 128],
                                IDENT[:, :]), inc="T")
                            t_tr = p.cnt["T"]
                            p.wait("vector", "T", t_tr)
                            dst = bass.AP(BIGt, O_VT + pr_i * 2080 + kt * 130, [[39296, P], [65, 2], [1, 64]])
                            p.run("vector", lambda e, dst=dst: e.tensor_copy(
                                dst, PT[:, 0:128].rearrange("p (a b) -> p a b", a=2)), inc="V")
                            state["pt_free_v"] = p.snap("V")
                state["proj_evac_v"] = p.snap("V")
            return p.snap("V")

        def attn_pair(g, pr_i, proj_v):
            pr = 2 * g + pr_i
            qb = big(O_Q + pr_i * T, T)
            kb = big(O_K + pr_i * T, T)
            vt0 = O_VT + pr_i * 2080
            ctx_base = O_CTX + pr * T
            p.wait("tensor", "V", proj_v)
            prev_exp = {0: 0, 1: 0}
            prev_av_t = 0
            v_prev_evac = 0
            pend_lch = [None]
            for qq in range(4):
                q0, q1 = qq * 512, (qq + 1) * 512
                ktmax = (qq + 1) * 4
                st_info = {}

                def emit_st(kt, qb=qb, kb=kb):
                    lo = max(q0, kt * P)
                    span = q1 - lo
                    par = kt % 2
                    stb_e = PB[4] if par == 0 else PB[0]
                    stb_o = PB[5] if par == 0 else PB[1]
                    exps_o = O_EXPS + par * 1024
                    p.wait("tensor", "S", prev_exp[par])
                    p.run("tensor", lambda e, kt=kt, lo=lo, q1=q1, qb=qb, kb=kb, stb_e=stb_e: e.matmul(
                        stb_e[:, 0:q1 - lo], kb[0:64, kt * P:(kt + 1) * P], qb[0:64, lo:q1],
                        start=True, stop=True, tile_position=(0, 0)))
                    p.run("tensor", lambda e, kt=kt, lo=lo, q1=q1, qb=qb, kb=kb, stb_o=stb_o: e.matmul(
                        stb_o[:, 0:q1 - lo], kb[64:128, kt * P:(kt + 1) * P], qb[64:128, lo:q1],
                        start=True, stop=True, tile_position=(64, 0)), inc="T")
                    t_st = p.cnt["T"]
                    p.wait("scalar", "T", t_st)
                    p.wait("scalar", "T", prev_av_t)
                    p.run("scalar", lambda e, exps_o=exps_o, span=span, stb_e=stb_e: e.activation(
                        big(exps_o, 512)[:, 0:span], stb_e[:, 0:span], AF.Exp, bias=-SOFT_SHIFT))
                    p.run("scalar", lambda e, exps_o=exps_o, span=span, stb_o=stb_o: e.activation(
                        big(exps_o + 512, 512)[:, 0:span], stb_o[:, 0:span], AF.Exp, bias=-SOFT_SHIFT), inc="S")
                    s_exp = p.cnt["S"]
                    prev_exp[par] = s_exp
                    state["last_exp_s"] = s_exp
                    v_mask = 0
                    if lo == kt * P:
                        p.wait("vector", "S", s_exp)
                        p.run("vector", lambda e, exps_o=exps_o: e.tensor_mul(
                            big(exps_o, 512)[:, 0:128], big(exps_o, 512)[:, 0:128], DIAGM[:, :]))
                        p.run("vector", lambda e, exps_o=exps_o: e.tensor_mul(
                            big(exps_o + 512, 512)[:, 0:128], big(exps_o + 512, 512)[:, 0:128], DIAGM[:, :]),
                            inc="V")
                        v_mask = p.snap("V")
                    st_info[kt] = (s_exp, v_mask, lo, span, exps_o)

                def emit_av(kt, ktmax=ktmax, vt0=vt0):
                    s_exp, v_mask, lo, span, exps_o = st_info[kt]
                    p.wait("tensor", "S", s_exp)
                    p.wait("tensor", "V", v_mask)
                    ca = lo - q0
                    p.run("tensor", lambda e, vt0=vt0, kt=kt, exps_o=exps_o, ca=ca, span=span, st=(kt == 0), sp=(kt == ktmax - 1): e.matmul(
                        PB[2][0:65, ca:ca + span], big(vt0 + kt * 130, 65), big(exps_o, 512)[:, 0:span],
                        start=st, stop=sp, skip_group_check=True))
                    p.run("tensor", lambda e, vt0=vt0, kt=kt, exps_o=exps_o, ca=ca, span=span, st=(kt == 0), sp=(kt == ktmax - 1): e.matmul(
                        PB[3][0:65, ca:ca + span], big(vt0 + kt * 130 + 65, 65), big(exps_o + 512, 512)[:, 0:span],
                        start=st, stop=sp, skip_group_check=True), inc="T")

                emit_st(0)
                if pend_lch[0] is not None:
                    pend_lch[0]()
                    pend_lch[0] = None
                p.wait("tensor", "V", v_prev_evac)
                for kt in range(ktmax):
                    if kt + 1 < ktmax:
                        emit_st(kt + 1)
                    emit_av(kt)
                    prev_av_t = p.cnt["T"]
                t_av = p.cnt["T"]

                def lch(t_av=t_av, q0=q0, q1=q1):
                    nonlocal v_prev_evac
                    for head, bank in ((0, PB[2]), (1, PB[3])):
                        p.wait("scalar", "T", t_av)
                        p.run("scalar", lambda e, bank=bank: e.activation(
                            SCRB[0:1, 0:512], bank[64:65, :], AF.Ln), inc="S")
                        s_ln = p.cnt["S"]
                        p.wait("tensor", "S", s_ln)
                        p.wait("tensor", "V", state.get("pxb_free_v", 0) if isinstance(state, dict) else 0)
                        p.run("tensor", lambda e: e.matmul(PXB[:, :], ONES1[0:1, :], SCRB[0:1, 0:512],
                                                           start=True, stop=True), inc="T")
                        t_lb = p.cnt["T"]
                        p.wait("scalar", "T", t_lb)
                        p.wait("scalar", "V", state["linv_used_v"])
                        p.run("scalar", lambda e: e.activation(SCRA[0:64, 0:512], PXB[0:64, :], AF.Exp, scale=-1.0),
                              inc="S")
                        s_li = p.cnt["S"]
                        p.wait("vector", "S", s_li)
                        if head == 0:
                            dst = big(ctx_base, T)[0:64, q0:q1]
                            p.run("vector", lambda e, dst=dst, bank=bank: e.tensor_mul(
                                dst, bank[0:64, :], SCRA[0:64, 0:512]), inc="V")
                            state["linv_used_v"] = p.snap("V")
                        else:
                            p.run("vector", lambda e, bank=bank: e.tensor_mul(
                                big(O_OST, 512)[0:64, :], bank[0:64, :], SCRA[0:64, 0:512]), inc="V")
                            v_st = p.snap("V")
                            state["linv_used_v"] = v_st
                            p.wait("tensor", "V", v_st)
                            p.run("tensor", lambda e: e.matmul(
                                PXB[:, :], ISH[:, :], big(O_OST, 512)[0:64, :],
                                start=True, stop=True), inc="T")
                            t_sh = p.cnt["T"]
                            p.wait("vector", "T", t_sh)
                            dst = big(ctx_base, T)[64:128, q0:q1]
                            p.run("vector", lambda e, dst=dst: e.tensor_copy(
                                dst, PXB[64:128, :]), inc="V")
                            state["pxb_free_v"] = p.snap("V")
                    v_prev_evac = p.snap("V")

                pend_lch[0] = lch
            pend_lch[0]()
            pend_lch[0] = None
            return p.snap("V")

        for _ in range(NCH):
            issue_wq(len(wq_marks))
        piece_order = [(part, tcp) for part in range(3) for tcp in range(2)]
        proj_done = {}
        for part, tcp in piece_order:
            proj_piece(0, part, tcp)
        proj_done[0] = p.snap("V")
        for g in range(4):
            for pr_i in range(2):
                attn_pair(g, pr_i, proj_done[g])
            if g < 3:
                for part, tcp in piece_order:
                    proj_piece(g + 1, part, tcp)
                proj_done[g + 1] = p.snap("V")

        ctx_v = p.snap("V")
        wo_marks = []
        wo_consumed = []

        def issue_wo(n):
            mc, kc = divmod(n, NCH)
            if n >= NCH:
                p.wait("gpsimd", "T", wo_consumed[n - NCH])
            else:
                p.wait("gpsimd", "T", p.cnt["T"])
            sname = "R%d" % (n % NCH)
            p.run("gpsimd", lambda e, mc=mc, kc=kc, n=n: e.dma_start(
                big(O_WOT + (n % NCH) * P, P), wout.ap()[kc * P:(kc + 1) * P, mc * P:(mc + 1) * P]), inc=sname)
            wo_marks.append((sname, p.snap(sname)))

        for _ in range(NCH):
            issue_wo(len(wo_marks))
        p.wait("tensor", "V", ctx_v)
        p.wait("tensor", "S", state["last_exp_s"])
        flv = [0, 0]
        for mc in range(NCH):
            while len(wo_marks) <= min((mc + 1) * NCH - 1, 8 * NCH - 1):
                issue_wo(len(wo_marks))
            for tc in range(4):
                tsl = slice(tc * 512, (tc + 1) * 512)
                i = mc * 4 + tc
                bank = PB[i % 2]
                p.wait("tensor", "V", flv[i % 2])
                for kc in range(NCH):
                    n = mc * NCH + kc
                    p.wait("tensor", wo_marks[n][0], wo_marks[n][1])
                    p.run("tensor", lambda e, n=n, kc=kc, tsl=tsl, bank=bank: e.matmul(
                        bank[:, :], big(O_WOT + (n % NCH) * P, P),
                        big(O_CTX + kc * T, T)[:, tsl], start=(kc == 0), stop=(kc == NCH - 1)),
                        inc="T" if (kc == NCH - 1 or tc == 3) else None)
                    if tc == 3:
                        wo_consumed.append(p.cnt["T"])
                t_op = p.cnt["T"]
                p.wait("vector", "T", t_op)
                dst = RX[:, mc * T:(mc + 1) * T][:, tsl]
                p.run("vector", lambda e, dst=dst, bank=bank, mc=mc: e.scalar_tensor_tensor(
                    dst, bank[:, :], BOUT[:, mc:mc + 1], dst, ALU.add, ALU.add), inc="V")
                flv[i % 2] = p.snap("V")
        return p.snap("V")

    O2_W1S, O2_W2S, O2_HT, O2_GW, O2_GATH, O2_ONES = 0, 16384, 32768, 33792, 34304, 36352

    def phase2(prev_v):
        LOGSB = SCRC
        GBC = SCRC
        for _eng in ("gpsimd", "sync", "scalar"):
            p.wait(_eng, "T", p.cnt["T"])
            p.wait(_eng, "V", p.cnt["V"])
            p.wait(_eng, "S", p.cnt["S"])
        v_ln = layer_norm(prev_v)
        for kc in range(NCH):
            p.run("gpsimd", lambda e, kc=kc: e.dma_start(
                big(O2_GW + kc * 32, NE), gwT.ap()[kc * P:(kc + 1) * P, :]), inc="W")
        w_gw = p.snap("W")
        p.wait("tensor", "W", w_gw)
        p.wait("tensor", "V", v_ln)
        v_logm = 0
        for tc in range(4):
            tsl = slice(tc * 512, (tc + 1) * 512)
            p.wait("tensor", "V", v_logm)
            for kc in range(NCH):
                p.run("tensor", lambda e, kc=kc, tsl=tsl: e.matmul(
                    PB[4][0:8, :], big(O2_GW + kc * 32, NE), XLN[:, kc * T:(kc + 1) * T][:, tsl],
                    start=(kc == 0), stop=(kc == NCH - 1)), inc="T" if kc == NCH - 1 else None)
            t_lg = p.cnt["T"]
            p.wait("vector", "T", t_lg)
            p.run("vector", lambda e, tsl=tsl: e.tensor_copy(LOGSB[0:8, tsl], PB[4][0:8, :]), inc="V")
            v_logm = p.snap("V")
        v_log = p.snap("V")
        p.wait("scalar", "V", v_log)
        p.run("scalar", lambda e: e.dma_start(ccg_in.ap().rearrange("(a b) -> a b", a=8), LOGSB[0:8, :]), inc="Q")
        q_st = p.snap("Q")
        p.wait("gpsimd", "Q", q_st)
        p.run("gpsimd", lambda e: e.collective_compute(
            "AllGather", ALU.bypass, RG, [ccg_in.ap()], [ccg_out.ap()]), inc="G")
        g_ag = p.snap("G")
        p.wait("gpsimd", "G", g_ag)
        p.run("gpsimd", lambda e: e.dma_start(
            bigf(O2_GATH, 1024), ccg_out.ap().rearrange("(a b) -> a b", a=P)), inc="cc0")
        w_gath = p.snap("cc0")

        w1_marks, w2_marks, q_consumed = [], [], {}

        def issue_quarter(eq):
            e_i, q_i = divmod(eq, 4)
            sl = eq % 2
            sname = "q%d" % sl
            if eq >= 2:
                p.wait("gpsimd", "T", q_consumed[eq - 2])
            for kc in range(NCH):
                p.run("gpsimd", lambda e, e_i=e_i, q_i=q_i, kc=kc, sl=sl: e.dma_start(
                    big(O2_W1S + sl * 8192 + kc * 1024, 1024),
                    w1T.ap()[e_i][kc * P:(kc + 1) * P, q_i * 1024:(q_i + 1) * 1024]), inc=sname)
            for hc in range(NCH):
                p.run("gpsimd", lambda e, e_i=e_i, q_i=q_i, hc=hc, sl=sl: e.dma_start(
                    big(O2_W2S + sl * 8192 + hc * 1024, 1024),
                    w2T.ap()[e_i][q_i * 1024 + hc * P:q_i * 1024 + (hc + 1) * P, :]), inc=sname)
            w1_marks.append((sname, p.snap(sname)))
            w2_marks.append((sname, p.snap(sname)))

        issue_quarter(0)
        issue_quarter(1)

        p.scope = "p2bis"
        p.wait("gpsimd", "cc0", w_gath)
        p.run("gpsimd", lambda e: e.memset(BIS[:, 0:1], -64.0))
        p.run("gpsimd", lambda e: e.memset(BIS[:, 1:2], 64.0))
        p.run("gpsimd", lambda e: e.memset(bigf(O2_ONES, 128), 1.0), inc="G")
        g_binit = p.snap("G")
        p.wait("vector", "G", g_binit)
        p.wait("vector", "cc0", w_gath)
        p.wait("tensor", "V", state.get("pxb_free_v", 0))
        for it in range(30):
            p.run("vector", lambda e: e.tensor_add(BIS[:, 2:3], BIS[:, 0:1], BIS[:, 1:2]))
            p.run("vector", lambda e: e.tensor_scalar(BIS[:, 2:3], BIS[:, 2:3], 0.5, None, ALU.mult))
            p.run("vector", lambda e: e.tensor_scalar(SCRA[:, 0:1024], bigf(O2_GATH, 1024), BIS[:, 2:3], None, ALU.is_lt))
            p.run("vector", lambda e: e.reduce_sum(BIS[:, 3:4], SCRA[:, 0:1024], AX.X), inc="V")
            v_c = p.snap("V")
            p.wait("tensor", "V", v_c)
            p.run("tensor", lambda e: e.matmul(PXB[:, 0:1], bigf(O2_ONES, 128), BIS[:, 3:4],
                                               start=True, stop=True), inc="T")
            t_tot = p.cnt["T"]
            p.wait("vector", "T", t_tot)
            p.run("vector", lambda e: e.tensor_copy(BIS[:, 4:5], PXB[:, 0:1]))
            p.run("vector", lambda e: e.tensor_scalar(BIS[:, 5:6], BIS[:, 4:5], float(CAP_K), None, ALU.is_lt))
            p.run("vector", lambda e: e.tensor_sub(BIS[:, 6:7], BIS[:, 2:3], BIS[:, 0:1]))
            p.run("vector", lambda e: e.tensor_mul(BIS[:, 6:7], BIS[:, 6:7], BIS[:, 5:6]))
            p.run("vector", lambda e: e.tensor_add(BIS[:, 0:1], BIS[:, 0:1], BIS[:, 6:7]))
            p.run("vector", lambda e: e.tensor_sub(BIS[:, 7:8], BIS[:, 2:3], BIS[:, 1:2]))
            p.run("vector", lambda e: e.tensor_mul(BIS[:, 8:9], BIS[:, 7:8], BIS[:, 5:6]))
            p.run("vector", lambda e: e.tensor_add(BIS[:, 1:2], BIS[:, 1:2], BIS[:, 7:8]))
            p.run("vector", lambda e: e.tensor_sub(BIS[:, 1:2], BIS[:, 1:2], BIS[:, 8:9]), inc="V")
        v_bis = p.snap("V")
        state["pxb_free_v"] = v_bis
        p.run("vector", lambda e: e.tensor_scalar(SCRA[0:8, 0:T], LOGSB[0:8, :], BIS[0:8, 0:1], None, ALU.is_ge))
        p.run("vector", lambda e: e.tensor_mul(LOGSB[0:8, :], LOGSB[0:8, :], SCRA[0:8, 0:T]), inc="V")
        v_g = p.snap("V")
        p.wait("scalar", "V", v_g)
        p.run("scalar", lambda e: e.dma_start(g_hbm.ap(), LOGSB[0:8, :]), inc="Q")
        q_g = p.snap("Q")

        gbc_marks = {}
        flush_done = {}

        def issue_gbc(e_i):
            p.wait("sync", "Q", q_g)
            p.wait("sync", "V", flush_done.get(e_i - 1, 0))
            p.run("sync", lambda e, e_i=e_i: e.dma_start(
                GBC[:, :], bass.AP(g_hbm, e_i * T, [[0, P], [1, T]])), inc="gb0")
            gbc_marks[e_i] = p.snap("gb0")

        p.scope = "p2exp"
        issue_gbc(0)
        p.wait("tensor", "V", v_log)
        gelu_done = {}
        ymm_done = {}
        flush_y_v = [0]
        for e_i in range(NE):
            if e_i >= 1:
                issue_gbc(e_i)
            for q_i in range(4):
                eq = e_i * 4 + q_i
                sl = eq % 2
                p.wait("tensor", w1_marks[eq][0], w1_marks[eq][1])
                for tb in range(NCH):
                    tb_sl = slice(tb * 256, (tb + 1) * 256)

                    def emit_h(hcp, sl=sl, tb_sl=tb_sl, tb=tb, eq=eq):
                        par = hcp % 2
                        hbank = PB[4 + par]
                        hidx = eq * 32 + tb * 4 + hcp
                        p.wait("tensor", "S", gelu_done.get(hidx - 2, 0))
                        for sub in range(2):
                            hc = 2 * hcp + sub
                            for kc in range(NCH):
                                p.run("tensor", lambda e, sl=sl, kc=kc, hc=hc, tb_sl=tb_sl, hbank=hbank, sub=sub: e.matmul(
                                    hbank[:, sub * 256:(sub + 1) * 256],
                                    big(O2_W1S + sl * 8192 + kc * 1024 + hc * P, P),
                                    XLN[:, kc * T:(kc + 1) * T][:, tb_sl],
                                    start=(kc == 0), stop=(kc == NCH - 1)),
                                    inc="T" if (kc == NCH - 1 and sub == 1) else None)
                        t_h = p.cnt["T"]
                        p.wait("scalar", "T", t_h)
                        p.wait("scalar", "T", ymm_done.get(hidx - 2, 0))
                        p.run("scalar", lambda e, hbank=hbank, par=par: e.activation(
                            big(O2_HT + par * 512, 512), hbank[:, :], AF.Gelu_apprx_tanh), inc="S")
                        gelu_done[hidx] = p.cnt["S"]

                    def emit_y(hcp, sl=sl, tb=tb, eq=eq):
                        par = hcp % 2
                        hidx = eq * 32 + tb * 4 + hcp
                        p.wait("tensor", "S", gelu_done[hidx])
                        if hcp == 0:
                            p.wait("tensor", "V", flush_y_v[0])
                        for sub in range(2):
                            hc = 2 * hcp + sub
                            for dd in range(NCH):
                                ybank = PB[dd // 2]
                                ycol = (dd % 2) * 256
                                p.run("tensor", lambda e, sl=sl, hc=hc, dd=dd, ybank=ybank, ycol=ycol, par=par, sub=sub,
                                      st=(hcp == 0 and sub == 0 and dd % 2 == 0),
                                      sp=(hcp == 3 and sub == 1 and dd % 2 == 1): e.matmul(
                                    ybank[:, ycol:ycol + 256],
                                    big(O2_W2S + sl * 8192 + hc * 1024 + dd * P, P),
                                    big(O2_HT + par * 512 + sub * 256, 256),
                                    start=st, stop=sp, skip_group_check=True),
                                    inc="T" if (sub == 1 and dd == NCH - 1) else None)
                        ymm_done[hidx] = p.cnt["T"]

                    emit_h(0)
                    for hcp in range(1, 4):
                        emit_h(hcp)
                        emit_y(hcp - 1)
                    emit_y(3)
                    t_y = p.cnt["T"]
                    p.wait("vector", "T", t_y)
                    p.wait("vector", "gb0", gbc_marks[e_i])
                    for j in range(4):
                        gsrc = bass.AP(SCRC, tb * 256, [[T, P], [0, 2], [1, 256]])
                        p.run("vector", lambda e, j=j, gsrc=gsrc: e.tensor_mul(
                            SCRA[:, 0:512], PB[j][:, :], gsrc))
                        rxap = bass.AP(RX, (2 * j) * T + tb * 256, [[NCH * T, P], [T, 2], [1, 256]])
                        p.run("vector", lambda e, rxap=rxap: e.tensor_add(rxap, rxap, SCRA[:, 0:512]),
                              inc="V" if j == 3 else None)
                    flush_y_v[0] = p.snap("V")
                    if q_i == 3 and tb == NCH - 1:
                        flush_done[e_i] = p.snap("V")
                q_consumed[eq] = p.cnt["T"]
                if eq + 2 < 32:
                    issue_quarter(eq + 2)
        return p.snap("V")

    O3_QK, O3_Q, O3_K, O3_VTOK, O3_WVB, O3_WQKB, O3_EX, O3_CTX3, O3_OHS, O3_SROW = \
        0, 2048, 4096, 6144, 22528, 24576, 25600, 27648, 29696, 32256

    def phase3(prev_v):
        for _eng in ("gpsimd", "sync", "scalar"):
            p.wait(_eng, "T", p.cnt["T"])
            p.wait(_eng, "V", p.cnt["V"])
            p.wait(_eng, "S", p.cnt["S"])
        p.scope = "p3ln"
        v_ln = layer_norm(prev_v)
        p.scope = "p3s2d"
        p.run("scalar", lambda e: e.dma_start(bigf(O3_OHS, 1280)[0:L1_NB, :], oh_c.ap()), inc="Q")
        p.run("scalar", lambda e: e.dma_start(bigf(O3_SROW, 1280)[0:1, :], maskadd_c.ap()), inc="Q")
        q_oh = p.snap("Q")
        for kc in range(NCH):
            p.run("gpsimd", lambda e, kc=kc: e.dma_start(
                big(O3_WQKB + kc * P, P), wqk.ap()[kc * P:(kc + 1) * P, :]), inc="W")
        w_qk = p.snap("W")
        p.wait("vector", "Q", q_pre)
        p.run("vector", lambda e: e.tensor_scalar(GAMT[:, 2:3], GAMT[:, 1:2], 1.0 / 32.0, None, ALU.mult), inc="V")
        v_gam = p.snap("V")
        p.run("vector", lambda e: e.tensor_scalar(LROW[:, 1:2], LROW[:, 0:1], math.sqrt(P), None, ALU.mult), inc="V")
        v_rel = p.snap("V")
        p.wait("tensor", "V", v_rel)
        p.wait("tensor", "Q", q_oh)
        for ch in range(3):
            a = ch * 512
            b = min(a + 512, 1280)
            p.wait("tensor", "V", p.cnt["V"] if ch > 0 else 0)
            p.run("tensor", lambda e, a=a, b=b: e.matmul(
                PXB[0:1, 0:b - a], LROW[:, 1:2], bigf(O3_OHS, 1280)[0:L1_NB, a:b],
                start=True, stop=True), inc="T")
            t_mm = p.cnt["T"]
            p.wait("vector", "T", t_mm)
            p.run("vector", lambda e, a=a, b=b: e.tensor_add(
                bigf(O3_SROW, 1280)[0:1, a:b], bigf(O3_SROW, 1280)[0:1, a:b], PXB[0:1, 0:b - a]), inc="V")
        v_srow = p.snap("V")
        p.wait("scalar", "V", v_srow)
        p.run("scalar", lambda e: e.dma_start(sext_hbm.ap()[None, :], bigf(O3_SROW, 1280)[0:1, :]), inc="Q")
        q_sx = p.snap("Q")
        p.wait("scalar", "Q", q_sx)
        for pp in range(P):
            p.run("scalar", lambda e, pp=pp: e.dma_start(
                SQBF[pp:pp + 1, 0:1024], bass.AP(sext_hbm, 127 - pp, [[1, 1], [1, 1024]])), inc="Q")
        q_s2d = p.snap("Q")

        p.scope = "p3qk"
        p.wait("tensor", "W", w_qk)
        p.wait("tensor", "V", v_ln)
        s_qkl = [0, 0]
        for tc in range(4):
            tsl = slice(tc * 512, (tc + 1) * 512)
            p.wait("tensor", "S", s_qkl[tc % 2])
            for kc in range(NCH):
                p.run("tensor", lambda e, kc=kc, tsl=tsl, tc=tc: e.matmul(
                    PB[4 + tc % 2][:, :], big(O3_WQKB + kc * P, P), XLN[:, kc * T:(kc + 1) * T][:, tsl],
                    start=(kc == 0), stop=(kc == NCH - 1)), inc="T" if kc == NCH - 1 else None)
            t_qk = p.cnt["T"]
            p.wait("scalar", "T", t_qk)
            p.run("scalar", lambda e, tsl=tsl, tc=tc: e.activation(
                big(O3_QK, 2048)[:, tsl], PB[4 + tc % 2][:, :], AF.Silu), inc="S")
            s_qkl[tc % 2] = p.snap("S")
        s_qk = p.snap("S")
        p.wait("vector", "S", s_qk)
        p.wait("vector", "V", v_gam)
        p.run("vector", lambda e: e.tensor_scalar(big(O3_Q, 2048), big(O3_QK, 2048), GAMT[:, 0:1], None, ALU.mult))
        p.run("vector", lambda e: e.tensor_scalar(big(O3_K, 2048), big(O3_QK, 2048), GAMT[:, 2:3], None, ALU.mult), inc="V")
        v_qkb = p.snap("V")

        p.scope = "p3vt"
        wv_marks = []
        wv_cons = []

        def issue_wv(n):
            kc = n % NCH
            if n >= 2:
                p.wait("gpsimd", "T", wv_cons[n - 2])
            sname = "wv%d" % (n % 2)
            p.run("gpsimd", lambda e, kc=kc, n=n: e.dma_start(
                big(O3_WVB + (n % 2) * 1024, 1024), wv.ap()[kc * P:(kc + 1) * P, :]), inc=sname)
            wv_marks.append((sname, p.snap(sname)))

        it_groups = [list(range(a, min(a + 3, 16))) for a in range(0, 16, 3)]
        s_vt = s_qk
        n_wv_total = len(it_groups) * NCH
        issue_wv(0)
        issue_wv(1)
        for gidx, grp in enumerate(it_groups):
            base_n = gidx * NCH
            for kci in range(NCH):
                kc = kci
                n = base_n + kci
                while len(wv_marks) <= min(n + 1, n_wv_total - 1):
                    issue_wv(len(wv_marks))
                p.wait("tensor", wv_marks[n][0], wv_marks[n][1])
                if kci == 0:
                    p.wait("tensor", "S", s_vt)
                    p.wait("tensor", "V", v_ln)
                for gi, it in enumerate(grp):
                    for vc in range(2):
                        p.run("tensor", lambda e, n=n, kc=kc, it=it, vc=vc, gi=gi: e.matmul(
                            PB[gi * 2 + vc][:, :], XLN[:, kc * T:(kc + 1) * T][:, it * P:(it + 1) * P],
                            big(O3_WVB + (n % 2) * 1024 + vc * 512, 512),
                            start=(kc == 0), stop=(kc == NCH - 1)),
                            inc="T" if (gi == len(grp) - 1 and vc == 1) else None)
                wv_cons.append(p.cnt["T"])
            t_v = p.cnt["T"]
            p.wait("scalar", "T", t_v)
            for gi, it in enumerate(grp):
                for vc in range(2):
                    p.run("scalar", lambda e, it=it, vc=vc, gi=gi: e.activation(
                        big(O3_VTOK + it * 1024 + vc * 512, 512), PB[gi * 2 + vc][:, :], AF.Silu),
                        inc="S" if (gi == len(grp) - 1 and vc == 1) else None)
            s_vt = p.snap("S")

        p.scope = "p3att"
        p.wait("vector", "Q", q_s2d)
        v_ctx = 0
        for ck in range(4):
            c0 = ck * 512
            s_ex = {}
            for jt in range(4):
                span = 512 - jt * P
                sb = PB[4 + jt % 2]
                p.wait("tensor", "S", s_ex.get(jt - 2, 0))
                p.wait("tensor", "V", v_qkb)
                p.wait("tensor", "S", s_vt)
                p.run("tensor", lambda e, c0=c0, jt=jt, span=span, sb=sb: e.matmul(
                    sb[:, 0:span], big(O3_K, 2048)[:, c0 + jt * P:c0 + (jt + 1) * P],
                    big(O3_Q, 2048)[:, c0 + jt * P:c0 + 512], start=True, stop=True), inc="T")
                t_s = p.cnt["T"]
                p.wait("vector", "T", t_s)
                p.run("vector", lambda e, span=span, sb=sb: e.tensor_add(
                    sb[:, 0:span], sb[:, 0:span], SQBF[:, 384:384 + span]), inc="V")
                v_b = p.snap("V")
                p.wait("scalar", "V", v_b)
                p.run("scalar", lambda e, jt=jt, span=span, sb=sb: e.activation(
                    big(O3_EX + jt * 512, 512)[:, 0:span], sb[:, 0:span], AF.Exp, bias=-L1_SHIFT), inc="S")
                s_ex[jt] = p.cnt["S"]
            for it in range(4):
                p.wait("tensor", "S", s_ex[it])
                p.wait("tensor", "V", v_ctx)
                for jt in range(it + 1):
                    lh = big(O3_EX + jt * 512, 512)[:, (it - jt) * P:(it - jt + 1) * P]
                    vt = big(O3_VTOK + (ck * 4 + jt) * 1024, 1024)
                    for vc in range(2):
                        p.run("tensor", lambda e, lh=lh, vt=vt, vc=vc, st=(jt == 0), sp=(jt == it): e.matmul(
                            PB[vc][:, :], lh, vt[:, vc * 512:(vc + 1) * 512],
                            start=st, stop=sp, skip_group_check=True))
                    p.run("tensor", lambda e, lh=lh, it=it, st=(jt == 0), sp=(jt == it): e.matmul(
                        PB[2][:, it:it + 1], lh, ONESB[:, 0:1],
                        start=st, stop=sp, skip_group_check=True),
                        inc="T" if jt == it else None)
                t_avx = p.cnt["T"]
                p.wait("vector", "T", t_avx)
                p.run("vector", lambda e, it=it: e.reciprocal(BIS[:, 8:9], PB[2][:, it:it + 1]))
                if dbg_t is not None:
                    p.run("vector", lambda e, ck=ck, it=it: e.tensor_copy(DQS[:, ck * 4 + it:ck * 4 + it + 1], PB[2][:, it:it + 1]))
                    p.run("vector", lambda e, ck=ck, it=it: e.tensor_copy(DQS[:, 32 + ck * 4 + it:32 + ck * 4 + it + 1], BIS[:, 8:9]))
                p.run("vector", lambda e: e.tensor_scalar(big(O3_CTX3, 1024)[:, 0:512], PB[0][:, :], BIS[:, 8:9], None, ALU.mult))
                p.run("vector", lambda e: e.tensor_scalar(big(O3_CTX3, 1024)[:, 512:1024], PB[1][:, :], BIS[:, 8:9], None, ALU.mult), inc="V")
                v_cs = p.snap("V")
                for c in range(NCH):
                    p.wait("tensor", "V", v_cs)
                    p.wait("tensor", "V", state["pt_free_v"])
                    p.run("tensor", lambda e, c=c: e.transpose(
                        PT[:, 0:128], big(O3_CTX3, 1024)[:, c * P:(c + 1) * P], IDENT[:, :]), inc="T")
                    t_tr = p.cnt["T"]
                    p.wait("vector", "T", t_tr)
                    rxs = RX[:, c * T:(c + 1) * T][:, c0 + it * P:c0 + (it + 1) * P]
                    p.run("vector", lambda e, rxs=rxs: e.tensor_add(rxs, rxs, PT[:, 0:128]), inc="V")
                    state["pt_free_v"] = p.snap("V")
                if dbg_t is not None:
                    p.wait("gpsimd", "V", v_cs)
                    p.run("gpsimd", lambda e, ck=ck, it=it: e.dma_start(
                        dbg_t.ap()[ck * 4 + it], big(O3_CTX3, 1024)), inc="cc0")
                    p.wait("gpsimd", "cc0", p.cnt["cc0"])
                    p.wait("vector", "cc0", p.cnt["cc0"])
                v_ctx = p.snap("V")
        return p.snap("V")

    O4_OB, O4_NW, O4_TR = 0, 32768, 36864

    def ob_chunk(ic):
        if ic < 8:
            return XLN[:, ic * T:(ic + 1) * T]
        return big(O4_OB + (ic - 8) * T, T)

    def phase4_real(prev_v):
        for _eng in ("gpsimd", "sync", "scalar"):
            p.wait(_eng, "T", p.cnt["T"])
            p.wait(_eng, "V", p.cnt["V"])
            p.wait(_eng, "S", p.cnt["S"])
        p.scope = "p4ln"
        v_ln = layer_norm(prev_v)
        p.scope = "p4mx"
        p.wait("vector", "V", v_ln)
        for c in range(NCH):
            p.run("vector", lambda e, c=c: e.reduce_max(MXW[:, c:c + 1], XLN[:, c * T:(c + 1) * T], AX.X,
                                                        apply_absolute_value=True))
        p.run("vector", lambda e: e.reduce_max(MXW[:, 8:9], MXW[:, 0:8], AX.X), inc="V")
        v_mx = p.snap("V")

        def cross_max(col, v_wait):
            p.wait("gpsimd", "V", v_wait)
            p.run("gpsimd", lambda e, col=col: e.dma_start(SCRB[0:1, 0:128], MXW[:, col:col + 1]), inc="cc0")
            mm = p.snap("cc0")
            p.wait("vector", "cc0", mm)
            p.run("vector", lambda e, col=col: e.reduce_max(MXW[0:1, col:col + 1], SCRB[0:1, 0:128], AX.X), inc="V")
            return p.snap("V")

        g_mx = cross_max(8, v_mx)

        def armax(idx, src_cols, wait_kind, wait_val):
            p.wait("vector", wait_kind, wait_val)
            for i in range(8):
                if i < len(src_cols):
                    c = src_cols[i]
                    p.run("vector", lambda e, i=i, c=c: e.tensor_copy(MXW[0:1, 16 + i:17 + i], MXW[0:1, c:c + 1]),
                          inc="V" if i == len(src_cols) - 1 else None)
                else:
                    break
            nz = len(src_cols)
            if nz < 8:
                p.run("vector", lambda e, nz=nz: e.tensor_scalar(
                    MXW[0:1, 16 + nz:24], MXW[0:1, 16 + nz:24], 0.0, None, ALU.mult), inc="V")
            v_stg = p.snap("V")
            p.wait("scalar", "V", v_stg)
            p.run("scalar", lambda e, idx=idx: e.dma_start(
                ccm_in[idx].ap().rearrange("(a b) -> a b", a=1)[:, 0:8], MXW[0:1, 16:24]), inc="Q")
            p.run("scalar", lambda e, idx=idx: e.dma_start(
                ccm_in[idx].ap().rearrange("(a b) -> a b", a=1)[:, 8:16], MXW[0:1, 16:24]), inc="Q")
            q_c = p.snap("Q")
            p.wait("gpsimd", "Q", q_c)
            p.run("gpsimd", lambda e, idx=idx: e.collective_compute(
                "AllReduce", ALU.max, RG, [ccm_in[idx].ap()], [ccm_out[idx].ap()]), inc="G")
            g_c = p.snap("G")
            p.wait("gpsimd", "G", g_c)
            p.run("gpsimd", lambda e, idx=idx: e.dma_start(
                MXW[:, 24:32], bass.AP(ccm_out[idx], 0, [[0, P], [1, 8]])), inc="cc0")
            return p.snap("cc0")

        w_ar0 = armax(0, [8], "V", g_mx)
        p.wait("vector", "cc0", w_ar0)
        p.run("vector", lambda e: e.reciprocal(MXW[:, 9:10], MXW[:, 24:25]))
        p.run("vector", lambda e: e.tensor_scalar(MXW[:, 10:11], MXW[:, 9:10], 32767.0, None, ALU.mult))
        p.run("vector", lambda e: e.tensor_scalar(MXW[:, 11:12], MXW[:, 24:25], 1.0 / 32767.0, None, ALU.mult), inc="V")
        v_r0 = p.snap("V")

        SI32 = SCRB[:, :].bitcast(I32)
        p.wait("vector", "V", v_r0)
        for c in range(NCH):
            p.run("vector", lambda e, c=c: e.tensor_scalar(
                SCRA[:, :], XLN[:, c * T:(c + 1) * T], MXW[:, 10:11], 32768.0, ALU.mult, ALU.add),
                inc="V")
            p.wait("scalar", "V", p.snap("V"))
            p.run("scalar", lambda e: e.activation(SI32[:, :], SCRA[:, :], AF.Copy))
            p.run("scalar", lambda e: e.activation(SQBF[:, :], SI32[:, :], AF.Copy), inc="S")
            p.wait("vector", "S", p.snap("S"))
            p.run("vector", lambda e: e.scalar_tensor_tensor(
                SCRB[:, :], SCRA[:, :], 0.5, SQBF[:, :], ALU.subtract, ALU.is_ge))
            p.run("vector", lambda e: e.tensor_tensor(SQBF[:, :], SQBF[:, :], SCRB[:, :], ALU.add))
            p.run("vector", lambda e, c=c: e.tensor_scalar(
                XLN[:, c * T:(c + 1) * T], SQBF[:, :], -32768.0, MXW[:, 11:12], ALU.add, ALU.mult),
                inc="V" if c == NCH - 1 else None)
        v_xq = p.snap("V")

        p.scope = "p4a"
        ON_NAT, ON_STG, ON_TR = 0, 4096, 8192
        nat_marks = []
        nat_cons = []
        nat_plan = []

        def issue_nat(n):
            h, r0_, cols = nat_plan[n]
            if n >= 2:
                p.wait("sync", "T", nat_cons[n - 2])
            sname = "na%d" % (n % 2)
            p.run("sync", lambda e, h=h, r0_=r0_, cols=cols, n=n: e.dma_start(
                bigf(ON_NAT + (n % 2) * 2048, 1024)[:, 0:cols], h.ap()[r0_:r0_ + P, 0:cols]), inc=sname)
            nat_marks.append((sname, p.snap(sname)))

        def dequant_transpose(n, cols, n_tiles, tr_slot):
            nat = bigf(ON_NAT + (n % 2) * 2048, 1024)[:, 0:cols]
            nb = cols // 32
            p.wait("vector", nat_marks[n][0], nat_marks[n][1])
            p.run("vector", lambda e, nat=nat, nb=nb: e.tensor_reduce(
                DQS[:, 0:nb], nat.rearrange("p (b i) -> p b i", i=32), AX.X, ALU.max,
                apply_absolute_value=True))
            p.run("vector", lambda e, nb=nb: e.reciprocal(DQS[:, 32:32 + nb], DQS[:, 0:nb]))
            p.run("vector", lambda e, nb=nb: e.tensor_scalar(DQS[:, 32:32 + nb], DQS[:, 32:32 + nb], 7.0, None, ALU.mult))
            p.run("vector", lambda e, nb=nb: e.tensor_scalar(DQS[:, 64:64 + nb], DQS[:, 0:nb], 1.0 / 7.0, None, ALU.mult))
            rinv_bc = bass.AP(DQS, 32, [[96, P], [1, nb], [0, 32]])
            s_bc = bass.AP(DQS, 64, [[96, P], [1, nb], [0, 32]])
            p.run("vector", lambda e, nat=nat, rinv_bc=rinv_bc, nb=nb: e.tensor_mul(
                SCRA[:, 0:nb * 32], nat, rinv_bc), inc="V")
            v_m = p.snap("V")
            p.wait("scalar", "V", v_m)
            p.run("scalar", lambda e, nb=nb: e.add(SI32[:, 0:nb * 32], SCRA[:, 0:nb * 32], 8.0))
            p.run("scalar", lambda e, nb=nb: e.activation(SCRC[:, 0:nb * 32], SI32[:, 0:nb * 32], AF.Copy, bias=-8.0),
                  inc="S")
            s_rt = p.snap("S")
            p.wait("vector", "S", s_rt)
            p.run("vector", lambda e, nb=nb: e.scalar_tensor_tensor(
                SCRB[:, 0:nb * 32], SCRA[:, 0:nb * 32], 0.5, SCRC[:, 0:nb * 32], ALU.subtract, ALU.is_ge))
            p.run("vector", lambda e, nb=nb: e.tensor_tensor(
                SCRC[:, 0:nb * 32], SCRC[:, 0:nb * 32], SCRB[:, 0:nb * 32], ALU.add))
            p.run("vector", lambda e, s_bc=s_bc, nb=nb: e.tensor_mul(
                SQBF[:, 0:nb * 32].bitcast(BF16)[:, 0:nb * 32], SCRC[:, 0:nb * 32], s_bc), inc="V")
            v_dq = p.snap("V")
            deq = SQBF[:, 0:cols].bitcast(BF16)[:, 0:cols]
            t_last = 0
            for j in range(n_tiles):
                p.wait("tensor", "V", v_dq)
                p.wait("tensor", "V", state["pt_free_v"])
                p.run("tensor", lambda e, deq=deq, j=j: e.transpose(
                    PT[:, 0:128], deq[:, j * P:(j + 1) * P], IDENT[:, :]), inc="T")
                t_tr = p.cnt["T"]
                p.wait("vector", "T", t_tr)
                p.run("vector", lambda e, j=j, tr_slot=tr_slot: e.tensor_copy(
                    big(ON_TR + tr_slot * 1024 + j * P, P), PT[:, 0:128]), inc="V")
                state["pt_free_v"] = p.snap("V")
            return p.snap("V")

        for h, _hb in ((wup_n, up_hbm), (wgate_n, gate_hbm)):
            for ic in range(24):
                nat_plan.append((h, ic * P, 1024))
        issue_nat(0)
        issue_nat(1)
        stg_v = [0, 0]
        stg_b = [0, 0]
        s_bank = [0, 0]
        for wi, (h, hbm_dst, mxcol) in enumerate(((wup_n, up_hbm, 12), (wgate_n, gate_hbm, 13))):
            for ic in range(24):
                n = wi * 24 + ic
                while len(nat_marks) <= min(n + 1, len(nat_plan) - 1):
                    issue_nat(len(nat_marks))
                tr_slot = n % 2
                v_tr = dequant_transpose(n, 1024, NCH, tr_slot)
                if _CONFIG.get("dbg4") and n == 0:
                    BGF = BIGt[:, :].bitcast(F32)
                    p.wait("sync", nat_marks[0][0], nat_marks[0][1])
                    p.wait("sync", "V", v_tr)
                    p.run("sync", lambda e: e.dma_start(dbg_t.ap()[2], bigf(ON_NAT, 1024)), inc="dbx")
                    p.wait("sync", "dbx", p.snap("dbx"))
                    p.run("sync", lambda e, BGF=BGF: e.dma_start(
                        dbg_t.ap()[0][:, 0:512], BGF[:, ON_TR // 2:ON_TR // 2 + 512]), inc="dbx")
                    p.wait("sync", "dbx", p.snap("dbx"))
                    p.run("sync", lambda e: e.dma_start(dbg_t.ap()[4][:, 0:96], DQS[:, :]), inc="dbx")
                    p.wait("sync", "dbx", p.snap("dbx"))
                    p.wait("sync", "V", v_xq)
                    p.run("sync", lambda e: e.dma_start(
                        dbg_t.ap()[3][:, 0:1024], XLNt[:, :].bitcast(F32)[:, 0:1024]), inc="dbx")
                    g_d = p.snap("dbx")
                    p.wait("sync", "dbx", g_d)
                    p.wait("vector", "dbx", g_d)
                p.wait("tensor", "V", v_tr)
                p.wait("tensor", "V", v_xq)
                sslot = n % 2
                p.wait("scalar", "st%d" % sslot, stg_b[sslot])
                for tc in range(4):
                    tsl = slice(tc * 512, (tc + 1) * 512)
                    p.wait("tensor", "S", s_bank[tc % 2])
                    for kc in range(NCH):
                        p.run("tensor", lambda e, kc=kc, tsl=tsl, tc=tc, tr_slot=tr_slot: e.matmul(
                            PB[tc % 2][:, :], big(ON_TR + tr_slot * 1024 + kc * P, P),
                            XLN[:, kc * T:(kc + 1) * T][:, tsl],
                            start=(kc == 0), stop=(kc == NCH - 1)), inc="T" if kc == NCH - 1 else None)
                    t_up = p.cnt["T"]
                    nat_cons.append(t_up) if tc == 3 else None
                    p.wait("scalar", "T", t_up)
                    p.run("scalar", lambda e, tsl=tsl, sslot=sslot, tc=tc: e.activation(
                        big(ON_STG + sslot * 2048, 2048)[:, tsl], PB[tc % 2][:, :], AF.Copy),
                        inc="S")
                    s_bank[tc % 2] = p.snap("S")
                s_slab = p.snap("S")
                if _CONFIG.get("dbg4") and n == 0:
                    p.wait("sync", "S", s_slab)
                    p.run("sync", lambda e: e.dma_start(
                        dbg_t.ap()[1][:, 0:1024],
                        BIGt[:, :].bitcast(F32)[:, ON_STG // 2:ON_STG // 2 + 1024]), inc="dbx")
                    g_d2 = p.snap("dbx")
                    p.wait("sync", "dbx", g_d2)
                    p.wait("scalar", "dbx", g_d2)
                p.wait("vector", "S", s_slab)
                p.run("vector", lambda e, sslot=sslot: e.reduce_max(
                    BIS[:, 9:10], big(ON_STG + sslot * 2048, 2048), AX.X, apply_absolute_value=True))
                p.run("vector", lambda e, mxcol=mxcol: e.tensor_max(
                    MXW[:, mxcol:mxcol + 1], MXW[:, mxcol:mxcol + 1], BIS[:, 9:10]), inc="V")
                stg_v[sslot] = p.snap("V")
                p.wait("sync", "V", stg_v[sslot])
                p.run("sync", lambda e, sslot=sslot, hbm_dst=hbm_dst, ic=ic: e.dma_start(
                    hbm_dst.ap()[ic * P:(ic + 1) * P, :], big(ON_STG + sslot * 2048, 2048)), inc="st%d" % sslot)
                stg_b[sslot] = p.snap("st%d" % sslot)
        s_4a_evac = p.snap("S")
        drain_4a = [("st0", p.snap("st0")), ("st1", p.snap("st1")),
                    ("na0", p.snap("na0")), ("na1", p.snap("na1")), ("T", p.cnt["T"])]

        v_fold = p.snap("V")
        g_ug = cross_max(12, v_fold)
        g_ug = cross_max(13, g_ug)
        w_ar2 = armax(1, [12, 13], "V", g_ug)
        p.wait("vector", "cc0", w_ar2)
        p.run("vector", lambda e: e.reciprocal(MXW[:, 9:10], MXW[:, 24:25]))
        p.run("vector", lambda e: e.tensor_scalar(MXW[:, 0:1], MXW[:, 9:10], 32767.0, None, ALU.mult))
        p.run("vector", lambda e: e.tensor_scalar(MXW[:, 1:2], MXW[:, 24:25], 1.0 / 32767.0, None, ALU.mult))
        p.run("vector", lambda e: e.reciprocal(MXW[:, 9:10], MXW[:, 25:26]))
        p.run("vector", lambda e: e.tensor_scalar(MXW[:, 2:3], MXW[:, 9:10], 32767.0, None, ALU.mult))
        p.run("vector", lambda e: e.tensor_scalar(MXW[:, 3:4], MXW[:, 25:26], 1.0 / 32767.0, None, ALU.mult), inc="V")
        v_r2 = p.snap("V")

        p.scope = "p4ba"
        gb_marks = []

        def issue_gslab(n, which, ic, slot):
            h = up_hbm if which == 0 else gate_hbm
            p.run("sync", lambda e, h=h, ic=ic, slot=slot: e.dma_start(
                SQBF[:, :].bitcast(BF16)[:, slot * 2048:(slot + 1) * 2048], h.ap()[ic * P:(ic + 1) * P, :]), inc="B")
            gb_marks.append(p.snap("B"))

        p.run("vector", lambda e: e.tensor_scalar(MXW[:, 14:15], MXW[:, 14:15], 0.0, None, ALU.mult), inc="V")
        for sn, vv in drain_4a:
            p.wait("vector", sn, vv)
            p.wait("sync", sn, vv)
        v_ob = {}
        for ic in range(24):
            gsl = (ic % 2) * 2048
            p.wait("sync", "V", v_ob.get(ic - 2, 0))
            rrn = "rr%d" % (ic % 2)
            p.run("sync", lambda e, ic=ic, gsl=gsl: e.dma_start(
                SQBF[:, :].bitcast(BF16)[:, gsl:gsl + 2048], gate_hbm.ap()[ic * P:(ic + 1) * P, :]), inc=rrn)
            b_g = p.snap(rrn)
            if _CONFIG.get("dbg4") and ic == 0:
                p.wait("sync", rrn, b_g)
                p.run("sync", lambda e: e.dma_start(dbg_t.ap()[7], SQBF[:, 0:1024]), inc="dbx")
                p.wait("sync", "dbx", p.snap("dbx"))
            p.wait("vector", rrn, b_g)
            p.wait("vector", "V", v_r2)
            gslab = SQBF[:, :].bitcast(BF16)[:, gsl:gsl + 2048]
            FZ = BIGt[:, :].bitcast(F32)[:, 16384:18432]
            p.run("vector", lambda e, gslab=gslab: e.tensor_scalar(SCRA[:, :], gslab, MXW[:, 2:3], 32768.0, ALU.mult, ALU.add),
                  inc="V")
            p.wait("scalar", "V", p.snap("V"))
            p.run("scalar", lambda e: e.activation(SI32[:, :], SCRA[:, :], AF.Copy))
            p.run("scalar", lambda e: e.activation(SCRC[:, :], SI32[:, :], AF.Copy), inc="S")
            p.wait("vector", "S", p.snap("S"))
            p.run("vector", lambda e: e.scalar_tensor_tensor(
                SCRB[:, :], SCRA[:, :], 0.5, SCRC[:, :], ALU.subtract, ALU.is_ge))
            p.run("vector", lambda e: e.tensor_tensor(SCRA[:, :], SCRC[:, :], SCRB[:, :], ALU.add))
            p.run("vector", lambda e: e.tensor_scalar(SCRA[:, :], SCRA[:, :], -32768.0, MXW[:, 3:4], ALU.add, ALU.mult), inc="V")
            v_gq = p.snap("V")
            p.wait("scalar", "V", v_gq)
            p.run("scalar", lambda e: e.activation(SCRC[:, :], SCRA[:, :], AF.Sigmoid), inc="S")
            s_sig = p.snap("S")
            p.wait("vector", "S", s_sig)
            p.run("vector", lambda e: e.tensor_scalar(SCRC[:, :], SCRC[:, :], 65536.0, None, ALU.mult), inc="V")
            p.wait("scalar", "V", p.snap("V"))
            p.run("scalar", lambda e: e.activation(SI32[:, :], SCRC[:, :], AF.Copy))
            p.run("scalar", lambda e, FZ=FZ: e.activation(FZ, SI32[:, :], AF.Copy), inc="S")
            p.wait("vector", "S", p.snap("S"))
            p.run("vector", lambda e, FZ=FZ: e.scalar_tensor_tensor(
                SCRB[:, :], SCRC[:, :], 0.5, FZ, ALU.subtract, ALU.is_ge))
            p.run("vector", lambda e, FZ=FZ: e.tensor_tensor(SCRC[:, :], FZ, SCRB[:, :], ALU.add))
            p.run("vector", lambda e: e.tensor_scalar(SCRC[:, :], SCRC[:, :], 65535.0, 1.0 / 65536.0, ALU.min, ALU.mult))
            dst = ob_chunk(ic)
            p.run("vector", lambda e, dst=dst: e.tensor_mul(dst, SCRA[:, :], SCRC[:, :]))
            p.run("vector", lambda e, dst=dst: e.reduce_max(BIS[:, 9:10], dst, AX.X, apply_absolute_value=True))
            p.run("vector", lambda e: e.tensor_max(MXW[:, 14:15], MXW[:, 14:15], BIS[:, 9:10]), inc="V")
            v_ob[ic] = p.snap("V")
        if _CONFIG.get("dbg4"):
            p.wait("sync", "V", v_ob[0])
            p.run("sync", lambda e: e.dma_start(
                dbg_t.ap()[6], XLNt[:, :].bitcast(F32)[:, 0:1024]), inc="dbx")
            db_gs = p.snap("dbx")
            p.wait("sync", "dbx", db_gs)
            p.wait("vector", "dbx", db_gs)
        g_g2 = cross_max(14, v_ob[23])
        w_ar3 = armax(2, [14], "V", g_g2)
        p.wait("vector", "cc0", w_ar3)
        p.run("vector", lambda e: e.reciprocal(MXW[:, 9:10], MXW[:, 24:25]))
        p.run("vector", lambda e: e.tensor_scalar(MXW[:, 4:5], MXW[:, 9:10], 32767.0, None, ALU.mult))
        p.run("vector", lambda e: e.tensor_scalar(MXW[:, 5:6], MXW[:, 24:25], 1.0 / 32767.0, None, ALU.mult))
        p.run("vector", lambda e: e.tensor_scalar(MXW[:, 15:16], MXW[:, 15:16], 0.0, None, ALU.mult), inc="V")
        v_r3 = p.snap("V")

        p.scope = "p4bb"
        v_o = {-2: v_ob[22], -1: v_ob[23]}
        for ic in range(24):
            usl = (ic % 2) * 2048
            p.wait("sync", "V", v_o.get(ic - 2, 0))
            rrn = "rr%d" % (ic % 2)
            p.run("sync", lambda e, ic=ic, usl=usl: e.dma_start(
                SQBF[:, :].bitcast(BF16)[:, usl:usl + 2048], up_hbm.ap()[ic * P:(ic + 1) * P, :]), inc=rrn)
            b_u = p.snap(rrn)
            if _CONFIG.get("dbg4") and ic == 0:
                p.wait("sync", rrn, b_u)
                p.run("sync", lambda e: e.dma_start(dbg_t.ap()[8], SQBF[:, 0:1024]), inc="dbx")
                p.wait("sync", "dbx", p.snap("dbx"))
            p.wait("vector", rrn, b_u)
            p.wait("vector", "V", v_r3)
            uslab = SQBF[:, :].bitcast(BF16)[:, usl:usl + 2048]
            FZ = BIGt[:, :].bitcast(F32)[:, 16384:18432]
            p.run("vector", lambda e, uslab=uslab: e.tensor_scalar(SCRA[:, :], uslab, MXW[:, 0:1], 32768.0, ALU.mult, ALU.add),
                  inc="V")
            p.wait("scalar", "V", p.snap("V"))
            p.run("scalar", lambda e: e.activation(SI32[:, :], SCRA[:, :], AF.Copy))
            p.run("scalar", lambda e: e.activation(SCRC[:, :], SI32[:, :], AF.Copy), inc="S")
            p.wait("vector", "S", p.snap("S"))
            p.run("vector", lambda e: e.scalar_tensor_tensor(
                SCRB[:, :], SCRA[:, :], 0.5, SCRC[:, :], ALU.subtract, ALU.is_ge))
            p.run("vector", lambda e: e.tensor_tensor(SCRA[:, :], SCRC[:, :], SCRB[:, :], ALU.add))
            p.run("vector", lambda e: e.tensor_scalar(SCRA[:, :], SCRA[:, :], -32768.0, MXW[:, 1:2], ALU.add, ALU.mult))
            src = ob_chunk(ic)
            p.run("vector", lambda e, src=src: e.tensor_scalar(SCRC[:, :], src, MXW[:, 4:5], 32768.0, ALU.mult, ALU.add),
                  inc="V")
            p.wait("scalar", "V", p.snap("V"))
            p.run("scalar", lambda e: e.activation(SI32[:, :], SCRC[:, :], AF.Copy))
            p.run("scalar", lambda e, FZ=FZ: e.activation(FZ, SI32[:, :], AF.Copy), inc="S")
            p.wait("vector", "S", p.snap("S"))
            p.run("vector", lambda e, FZ=FZ: e.scalar_tensor_tensor(
                SCRB[:, :], SCRC[:, :], 0.5, FZ, ALU.subtract, ALU.is_ge))
            p.run("vector", lambda e, FZ=FZ: e.tensor_tensor(SCRC[:, :], FZ, SCRB[:, :], ALU.add))
            p.run("vector", lambda e: e.tensor_scalar(SCRC[:, :], SCRC[:, :], -32768.0, MXW[:, 5:6], ALU.add, ALU.mult))
            p.run("vector", lambda e, src=src: e.tensor_mul(src, SCRC[:, :], SCRA[:, :]))
            p.run("vector", lambda e, src=src: e.reduce_max(BIS[:, 9:10], src, AX.X, apply_absolute_value=True))
            p.run("vector", lambda e: e.tensor_max(MXW[:, 15:16], MXW[:, 15:16], BIS[:, 9:10]), inc="V")
            v_o[ic] = p.snap("V")
        g_o = cross_max(15, v_o[23])
        w_ar4 = armax(3, [15], "V", g_o)
        p.wait("vector", "cc0", w_ar4)
        p.run("vector", lambda e: e.reciprocal(MXW[:, 9:10], MXW[:, 24:25]))
        p.run("vector", lambda e: e.tensor_scalar(MXW[:, 6:7], MXW[:, 9:10], 32767.0, None, ALU.mult))
        p.run("vector", lambda e: e.tensor_scalar(MXW[:, 7:8], MXW[:, 24:25], 1.0 / 32767.0, None, ALU.mult), inc="V")
        v_r4 = p.snap("V")

        p.scope = "p4bc"
        v_oq = {}
        p.wait("vector", "V", v_r4)
        for ic in range(24):
            src = ob_chunk(ic)
            p.run("vector", lambda e, src=src: e.tensor_scalar(SCRA[:, :], src, MXW[:, 6:7], 32768.0, ALU.mult, ALU.add),
                  inc="V")
            p.wait("scalar", "V", p.snap("V"))
            p.run("scalar", lambda e: e.activation(SI32[:, :], SCRA[:, :], AF.Copy))
            p.run("scalar", lambda e: e.activation(SCRC[:, :], SI32[:, :], AF.Copy), inc="S")
            p.wait("vector", "S", p.snap("S"))
            p.run("vector", lambda e: e.scalar_tensor_tensor(
                SCRB[:, :], SCRA[:, :], 0.5, SCRC[:, :], ALU.subtract, ALU.is_ge))
            p.run("vector", lambda e: e.tensor_tensor(SCRA[:, :], SCRC[:, :], SCRB[:, :], ALU.add))
            p.run("vector", lambda e, src=src: e.tensor_scalar(src, SCRA[:, :], -32768.0, MXW[:, 7:8], ALU.add, ALU.mult),
                  inc="V")
            v_oq[ic] = p.snap("V")
        v_oq_all = p.snap("V")

        p.scope = "p4c"
        nat_plan2 = []
        for oc in range(NCH):
            for j in range(3):
                nat_plan2.append((oc, j))
        nat2_marks = []
        nat2_cons = []

        def issue_nat2(n):
            oc, j = nat_plan2[n]
            if n >= 2:
                p.wait("sync", "T", nat2_cons[n - 2])
            sname = "na%d" % (n % 2)
            p.run("sync", lambda e, oc=oc, j=j, n=n: e.dma_start(
                bigf(32768 + (n % 2) * 2048, 1024), wdown_n.ap()[oc * P:(oc + 1) * P, j * 1024:(j + 1) * 1024]), inc=sname)
            nat2_marks.append((sname, p.snap(sname)))

        def dequant_transpose2(n, tr_slot):
            nat = bigf(32768 + (n % 2) * 2048, 1024)
            nb = 32
            p.wait("vector", nat2_marks[n][0], nat2_marks[n][1])
            p.run("vector", lambda e, nat=nat: e.tensor_reduce(
                DQS[:, 0:32], nat.rearrange("p (b i) -> p b i", i=32), AX.X, ALU.max,
                apply_absolute_value=True))
            p.run("vector", lambda e: e.reciprocal(DQS[:, 32:64], DQS[:, 0:32]))
            p.run("vector", lambda e: e.tensor_scalar(DQS[:, 32:64], DQS[:, 32:64], 7.0, None, ALU.mult))
            p.run("vector", lambda e: e.tensor_scalar(DQS[:, 64:96], DQS[:, 0:32], 1.0 / 7.0, None, ALU.mult))
            rinv_bc = bass.AP(DQS, 32, [[96, P], [1, 32], [0, 32]])
            s_bc = bass.AP(DQS, 64, [[96, P], [1, 32], [0, 32]])
            p.run("vector", lambda e, nat=nat, rinv_bc=rinv_bc: e.tensor_mul(SCRA[:, 0:1024], nat, rinv_bc), inc="V")
            v_m = p.snap("V")
            p.wait("scalar", "V", v_m)
            p.run("scalar", lambda e: e.add(SI32[:, 0:1024], SCRA[:, 0:1024], 8.0))
            p.run("scalar", lambda e: e.activation(SCRC[:, 512:1536], SI32[:, 0:1024], AF.Copy, bias=-8.0), inc="S")
            s_rt = p.snap("S")
            p.wait("vector", "S", s_rt)
            p.run("vector", lambda e: e.scalar_tensor_tensor(
                SCRB[:, 0:1024], SCRA[:, 0:1024], 0.5, SCRC[:, 512:1536], ALU.subtract, ALU.is_ge))
            p.run("vector", lambda e: e.tensor_tensor(
                SCRC[:, 512:1536], SCRC[:, 512:1536], SCRB[:, 0:1024], ALU.add))
            p.run("vector", lambda e, s_bc=s_bc: e.tensor_mul(
                SCRC[:, 0:512].bitcast(BF16)[:, 0:1024], SCRC[:, 512:1536], s_bc), inc="V")
            v_dq = p.snap("V")
            deq = SCRC[:, 0:512].bitcast(BF16)[:, 0:1024]
            for j in range(NCH):
                p.wait("tensor", "V", v_dq)
                p.wait("tensor", "V", state["pt_free_v"])
                p.run("tensor", lambda e, deq=deq, j=j: e.transpose(
                    PT[:, 0:128], deq[:, j * P:(j + 1) * P], IDENT[:, :]), inc="T")
                t_tr = p.cnt["T"]
                p.wait("vector", "T", t_tr)
                p.run("vector", lambda e, j=j, tr_slot=tr_slot: e.tensor_copy(
                    big(36864 + tr_slot * 1024 + j * P, P), PT[:, 0:128]), inc="V")
                state["pt_free_v"] = p.snap("V")
            return p.snap("V")

        p.wait("sync", "V", v_o[23])
        issue_nat2(0)
        issue_nat2(1)
        v_dn = [0, 0, 0, 0]
        p.wait("tensor", "S", s_4a_evac)
        for oc in range(NCH):
            for j in range(3):
                n = oc * 3 + j
                while len(nat2_marks) <= min(n + 1, 23):
                    issue_nat2(len(nat2_marks))
                v_tr = dequant_transpose2(n, n % 2)
                p.wait("tensor", "V", v_tr)
                p.wait("tensor", "V", v_oq_all)
                for tc in range(4):
                    tsl = slice(tc * 512, (tc + 1) * 512)
                    if j == 0:
                        p.wait("tensor", "V", v_dn[tc])
                    for kci in range(NCH):
                        kc = 8 * j + kci
                        p.run("tensor", lambda e, kc=kc, kci=kci, tsl=tsl, tc=tc, j=j, n=n: e.matmul(
                            PB[tc][:, :], big(36864 + (n % 2) * 1024 + kci * P, P),
                            ob_chunk(kc)[:, tsl],
                            start=(j == 0 and kci == 0), stop=(j == 2 and kci == NCH - 1)),
                            inc="T" if kci == NCH - 1 else None)
                    if tc == 3:
                        nat2_cons.append(p.cnt["T"])
                    if j == 2:
                        t_dn = p.cnt["T"]
                        p.wait("vector", "T", t_dn)
                        dst = RX[:, oc * T:(oc + 1) * T][:, tsl]
                        p.run("vector", lambda e, dst=dst, tc=tc: e.tensor_add(dst, dst, PB[tc][:, :]), inc="V")
                        v_dn[tc] = p.snap("V")
        if _CONFIG.get("dbg4"):
            p.wait("sync", "V", p.cnt["V"])
            p.run("sync", lambda e: e.dma_start(dbg_t.ap()[5][:, 0:32], MXW[:, :]), inc="dbx")
            p.wait("sync", "dbx", p.snap("dbx"))
            p.run("sync", lambda e: e.dma_start(dbg_t.ap()[5][:, 32:48], BIS[:, :]), inc="dbx")
            p.wait("sync", "dbx", p.snap("dbx"))
            for k in range(6):
                src = XLNt[:, :].bitcast(F32)[:, k * 1024:(k + 1) * 1024]
                p.run("sync", lambda e, k=k, src=src: e.dma_start(dbg_t.ap()[10 + k], src), inc="dbx")
                p.wait("sync", "dbx", p.snap("dbx"))
        return p.snap("V")

    def store_out(prev_v):
        for _eng in ("gpsimd", "sync", "scalar"):
            p.wait(_eng, "T", p.cnt["T"])
            p.wait(_eng, "V", p.cnt["V"])
            p.wait(_eng, "S", p.cnt["S"])
        p.wait("tensor", "V", prev_v)
        vq = 0
        b_marks = {}
        for it in range(16):
            slot = (it % 2) * 1024
            for c in range(NCH):
                p.wait("tensor", "V", state["pt_free_v"])
                if c == 0:
                    p.wait("vector", "st%d" % (it % 2), b_marks.get(it - 2, 0))
                p.run("tensor", lambda e, it=it, c=c: e.transpose(
                    PXB[:, 0:128], RX[:, c * T:(c + 1) * T][:, it * P:(it + 1) * P], IDENTF[:, :]), inc="T")
                t_tr = p.cnt["T"]
                p.wait("vector", "T", t_tr)
                p.run("vector", lambda e, it=it, c=c, slot=slot: e.tensor_copy(
                    SCRB[:, slot + c * P:slot + (c + 1) * P], PXB[:, 0:128]), inc="V")
                state["pt_free_v"] = p.snap("V")
            v_row = p.snap("V")
            p.wait("sync", "V", v_row)
            p.run("sync", lambda e, it=it, slot=slot: e.dma_start(
                out_t.ap()[it * P:(it + 1) * P, :], SCRB[:, slot:slot + 1024]), inc="st%d" % (it % 2))
            b_marks[it] = p.snap("st%d" % (it % 2))
        for sname in ("B", "W", "Q", "st0", "st1", "gb0", "cc0", "rr0", "rr1", "q0", "q1",
                      "wv0", "wv1", "na0", "na1") + tuple("R%d" % i for i in range(8)):
            p.wait("sync", sname, p.snap(sname))

    p.scope = "p1"
    v1 = phase1()
    if phases >= 2:
        p.scope = "p2"
        v1 = phase2(v1)
    if phases >= 3:
        p.scope = "p3"
        v1 = phase3(v1)
    if phases >= 4:
        p.scope = "p4"
        v1 = phase4_real(v1)
    p.scope = "st"
    store_out(v1)

    p.finalize(nc)
    for cm in reversed(cms):
        cm.__exit__(None, None, None)
    return nc


def _host_constants():
    ident = np.eye(P, dtype=ml_dtypes.bfloat16)
    diagm = np.triu(np.ones((P, P), dtype=np.float32)).astype(ml_dtypes.bfloat16)
    ish = np.zeros((64, P), dtype=ml_dtypes.bfloat16)
    for k in range(64):
        ish[k, k + 64] = 1.0
    oh = np.zeros((L1_NB, 1280), dtype=np.float32)
    maskadd = np.full((1, 1280), NEG_BIG, dtype=np.float32)
    for idx in range(1280):
        n = idx - 511
        if 0 <= n <= 511:
            oh[_bucket_of(n), idx] = 1.0
            maskadd[0, idx] = 0.0
    consts = np.zeros(32, dtype=np.float32)
    consts[:len(CONST_VALS)] = np.asarray(CONST_VALS, dtype=np.float32)
    return ident, diagm, ish, oh, maskadd, consts


def make_in_maps(L0_attn_x, L0_attn_Wqkv, L0_attn_bqkv, L0_attn_Wout, L0_attn_bout,
                 L0_ffn_gate_weight, L0_ffn_w1, L0_ffn_b1, L0_ffn_w2, L0_ffn_b2,
                 L1_attn_Wqk, L1_attn_bqk, L1_attn_Wv, L1_attn_bv, L1_attn_gamma,
                 L1_attn_beta, L1_attn_rel_emb, L1_ffn_w_gate, L1_ffn_w_up, L1_ffn_w_down):
    x = np.asarray(L0_attn_x)
    ident, diagm, ish, oh, maskadd, consts = _host_constants()
    shared = dict(
        wqkv=np.ascontiguousarray(np.asarray(L0_attn_Wqkv), dtype=np.float32),
        bqkv_r=np.ascontiguousarray(np.asarray(L0_attn_bqkv).reshape(24, P).T),
        wout=np.ascontiguousarray(np.asarray(L0_attn_Wout), dtype=np.float32),
        bout_r=np.ascontiguousarray(np.asarray(L0_attn_bout).reshape(8, P).T),
        gwT=np.ascontiguousarray(np.asarray(L0_ffn_gate_weight).T),
        w1T=np.ascontiguousarray(np.asarray(L0_ffn_w1).transpose(0, 2, 1)),
        w2T=np.ascontiguousarray(np.asarray(L0_ffn_w2).transpose(0, 2, 1)),
        wqk=np.ascontiguousarray(np.asarray(L1_attn_Wqk)),
        wv=np.ascontiguousarray(np.asarray(L1_attn_Wv)),
        gammaT=np.ascontiguousarray(np.asarray(L1_attn_gamma).T),
        rel_in=np.ascontiguousarray(np.asarray(L1_attn_rel_emb)),
        wup_n=np.ascontiguousarray(np.asarray(L1_ffn_w_up)),
        wgate_n=np.ascontiguousarray(np.asarray(L1_ffn_w_gate)),
        wdown_n=np.ascontiguousarray(np.asarray(L1_ffn_w_down)),
        ident_c=ident, diagm_c=diagm, ishift_c=ish, oh_c=oh, maskadd_c=maskadd,
        consts_c=consts,
    )
    in_maps = []
    for c in range(8):
        im = dict(shared)
        im["xT"] = np.ascontiguousarray(x[c].T)
        in_maps.append(im)
    return in_maps


def kernel(**inputs):
    nc = build_nc(_CONFIG["phases"])
    in_maps = make_in_maps(**inputs)
    res = run_bass_kernel_spmd(nc, in_maps, core_ids=list(range(8)))
    out = np.stack([res.results[c]["out"] for c in range(8)], axis=0)
    return out


# version 08 (final)
# speedup vs baseline: 1.0819x; 1.0819x over previous
import math
import numpy as np
import ml_dtypes
import concourse.bass as bass
import concourse.mybir as mybir
from concourse.bass_isa import ReduceOp
from concourse.bass_utils import run_bass_kernel_spmd

F32 = mybir.dt.float32
BF16 = mybir.dt.bfloat16
I32 = mybir.dt.int32
AF = mybir.ActivationFunctionType
ALU = mybir.AluOpType
AX = mybir.AxisListType

P = 128
T = 2048
D = 1024
NCH = 8
FF = 4096
NE = 8
CAP_K = 24576
SOFT_SHIFT = 20.0
L1_SHIFT = 12.0
L1_NB = 32
L1_MD = 128
L1_I = 3072
NEG_BIG = -1e30
RG = [[0, 1, 2, 3, 4, 5, 6, 7]]

_CONFIG = {"phases": 4}

CONST_VALS = [0.0, 1.0, -SOFT_SHIFT, -L1_SHIFT, 1e-5, -0.5, 0.5, -8.0,
              1.0 / 1024.0, 0.125, -32768.0, 1.0 / 65536.0, 65535.0, 32767.0,
              32768.5, 8.5, float(CAP_K), 2.0, -1.0, -2.0, math.sqrt(P),
              1.0 / 7.0, 7.0, 65536.0, 1.0 / 32767.0, -65536.0, 8.0]


def _bucket_of(n):
    me = L1_NB // 2
    if n < me:
        return n
    v = me + math.log(max(n, 1) / me) / math.log(L1_MD / me) * (L1_NB - me)
    return int(min(v, L1_NB - 1))


class Prog:
    ENGS = ("sync", "tensor", "vector", "scalar", "gpsimd")

    def __init__(self):
        self.ops = {e: [] for e in self.ENGS}
        self.cnt = {"T": 0, "V": 0, "S": 0, "G": 0, "W": 0, "B": 0, "Q": 0}
        self.sem = {}
        self.dma_semnames = {"W", "B", "Q"}
        self.scope = None

    def emit(self, eng, fn):
        self.ops[eng].append((self.scope, fn))

    def wait(self, eng, sem_name, val):
        if val is None or val <= 0:
            return
        s = self.sem[sem_name]
        mult = 16 if sem_name in self.dma_semnames else 1
        self.emit(eng, lambda e, s=s, v=mult * val: e.wait_ge(s, v))

    AUTO = {"vector": "V", "scalar": "S", "gpsimd": "G"}

    def run(self, eng, builder, inc=None):
        is_dma = inc in self.dma_semnames
        auto = self.AUTO.get(eng)
        if auto is not None and not is_dma:
            prev = self.cnt[auto]
            sa = self.sem[auto]
            self.emit(eng, lambda e, b=builder, sa=sa, prev=prev: (
                e.wait_ge(sa, prev) if prev > 0 else None,
                b(e).then_inc(sa, 1)))
            self.cnt[auto] += 1
            return self.cnt[auto]
        if inc is None:
            self.emit(eng, lambda e, b=builder: b(e))
            return None
        s = self.sem[inc]
        mult = 16 if is_dma else 1
        self.emit(eng, lambda e, b=builder, s=s, m=mult: b(e).then_inc(s, m))
        self.cnt[inc] += 1
        return self.cnt[inc]

    def add_dma_sem(self, name, handle):
        self.sem[name] = handle
        self.cnt[name] = 0
        self.dma_semnames.add(name)

    def snap(self, name):
        return self.cnt[name]

    def finalize(self, nc):
        with nc.Block() as block:
            for eng in self.ENGS:
                lst = self.ops[eng]
                if not lst:
                    continue
                def runner(e, lst=lst):
                    for scope, fn in lst:
                        if scope is None:
                            fn(e)
                        else:
                            with nc.named_scope(scope):
                                fn(e)
                getattr(block, eng)(runner)


def build_nc(phases=4):
    nc = bass.Bass("TRN2", target_bir_lowering=False)

    def din(name, shape, dtype=F32):
        return nc.dram_tensor(name, list(shape), dtype, kind="ExternalInput")

    xT = din("xT", (D, T))
    wqkv = din("wqkv", (D, 3 * D))
    bqkv_r = din("bqkv_r", (P, 24))
    wout = din("wout", (D, D))
    bout_r = din("bout_r", (P, 8))
    gwT = din("gwT", (D, NE))
    w1T = din("w1T", (NE, D, FF))
    w2T = din("w2T", (NE, FF, D))
    wqk = din("wqk", (D, P))
    wv = din("wv", (D, D))
    gammaT = din("gammaT", (P, 2))
    rel_in = din("rel_in", (L1_NB, 1))
    wup_n = din("wup_n", (L1_I, D))
    wgate_n = din("wgate_n", (L1_I, D))
    wdown_n = din("wdown_n", (D, L1_I))
    ident_c = din("ident_c", (P, P), BF16)
    diagm_c = din("diagm_c", (P, P), BF16)
    ishift_c = din("ishift_c", (64, P), BF16)
    oh_c = din("oh_c", (L1_NB, 1280))
    maskadd_c = din("maskadd_c", (1, 1280))
    consts_c = din("consts_c", (32,))

    out_t = nc.dram_tensor("out", [T, D], F32, kind="ExternalOutput")
    dbg_t = nc.dram_tensor("dbg", [16, P, D], F32, kind="ExternalOutput") \
        if (_CONFIG.get("dbg3") or _CONFIG.get("dbg4")) else None
    g_hbm = nc.dram_tensor("g_hbm", [NE, T], F32, kind="Internal")
    sext_hbm = nc.dram_tensor("sext_hbm", [1280], F32, kind="Internal")
    ccg_in = nc.dram_tensor("ccg_in", [NE * T], F32, kind="Internal")
    ccg_out = nc.dram_tensor("ccg_out", [8 * NE * T], F32, kind="Internal")
    ccm_in = [nc.dram_tensor(f"ccm{i}_in", [16], F32, kind="Internal") for i in range(4)]
    ccm_out = [nc.dram_tensor(f"ccm{i}_out", [16], F32, kind="Internal") for i in range(4)]
    bis_hbm = nc.dram_tensor("bis_hbm", [16], F32, kind="Internal")
    up_hbm = nc.dram_tensor("up_hbm", [L1_I, T], BF16, kind="Internal")
    gate_hbm = nc.dram_tensor("gate_hbm", [L1_I, T], BF16, kind="Internal")

    cms = []

    def enter(cm):
        cms.append(cm)
        return cm.__enter__()

    RX = enter(nc.sbuf_tensor("RX", [P, NCH * T], F32))
    XLNt = enter(nc.sbuf_tensor("XLN", [P, NCH * T], BF16))
    XLN = XLNt[:, :]
    SCRA = enter(nc.sbuf_tensor("SCRA", [P, T], F32))
    SCRB = enter(nc.sbuf_tensor("SCRB", [P, T], F32))
    SQBF = enter(nc.sbuf_tensor("SQBF", [P, T], F32))
    SCRC = enter(nc.sbuf_tensor("SCRC", [P, T], F32))
    CB = enter(nc.sbuf_tensor("CB", [P, 32], F32))
    ONESB = enter(nc.sbuf_tensor("ONESB", [P, 16], BF16))
    ONES1 = enter(nc.sbuf_tensor("ONES1", [1, P], F32))
    IDENT = enter(nc.sbuf_tensor("IDENT", [P, P], BF16))
    IDENTF = enter(nc.sbuf_tensor("IDENTF", [P, P], F32))
    DIAGM = enter(nc.sbuf_tensor("DIAGM", [P, P], BF16))
    ISH = enter(nc.sbuf_tensor("ISH", [64, P], BF16))
    BQKV = enter(nc.sbuf_tensor("BQKV", [P, 24], F32))
    BOUT = enter(nc.sbuf_tensor("BOUT", [P, 8], F32))
    GAMT = enter(nc.sbuf_tensor("GAMT", [P, 8], F32))
    LROW = enter(nc.sbuf_tensor("LROW", [L1_NB, 4], F32))
    MXW = enter(nc.sbuf_tensor("MXW", [P, 32], F32))
    BIS = enter(nc.sbuf_tensor("BIS", [P, 16], F32))
    DQS = enter(nc.sbuf_tensor("DQS", [P, 96], F32))
    BIGt = enter(nc.sbuf_tensor("BIG", [P, 39296], BF16))
    BIG = BIGt[:, :]

    def big(o, cols):
        assert o + cols <= 39296, (o, cols)
        return BIG[:, o:o + cols]

    def bigf(o, n):
        return BIG[:, o:o + 2 * n].bitcast(F32)

    PB = [enter(nc.psum_tensor(f"pb{i}", [P, 512], F32)) for i in range(6)]
    PXB = enter(nc.psum_tensor("pxb", [P, 512], F32))
    PT = enter(nc.psum_tensor("pt", [P, 1024], BF16))

    p = Prog()
    for s in ("T", "V", "S", "G", "W", "B", "Q"):
        p.sem[s] = enter(nc.semaphore(s))
    RINGSEMS = ["R%d" % i for i in range(8)] + ["q0", "q1", "wv0", "wv1", "na0", "na1",
                "st0", "st1", "rr0", "rr1", "gb0", "cc0", "dbx"]
    for s in RINGSEMS:
        p.add_dma_sem(s, enter(nc.semaphore(s)))

    for i, v in enumerate(CONST_VALS):
        nc.const_aps.register(F32, float(v), CB[:, i:i + 1])

    for args in [
        lambda e: e.dma_start(CB[:, 0:32], bass.AP(consts_c, 0, [[0, P], [1, 32]])),
        lambda e: e.dma_start(IDENT[:, :], ident_c.ap()),
        lambda e: e.dma_start(DIAGM[:, :], diagm_c.ap()),
        lambda e: e.dma_start(ISH[:, :], ishift_c.ap()),
        lambda e: e.dma_start(BQKV[:, :], bqkv_r.ap()),
        lambda e: e.dma_start(BOUT[:, :], bout_r.ap()),
        lambda e: e.dma_start(GAMT[:, 0:2], gammaT.ap()),
        lambda e: e.dma_start(LROW[:, 0:1], rel_in.ap()),
    ]:
        p.run("scalar", args, inc="Q")
    q_pre = p.snap("Q")
    for c in range(NCH):
        p.run("sync", lambda e, c=c: e.dma_start(RX[:, c * T:(c + 1) * T], xT.ap()[c * P:(c + 1) * P, :]), inc="B")
    b_x = p.snap("B")
    p.wait("gpsimd", "Q", q_pre)
    p.wait("scalar", "Q", q_pre)
    p.wait("vector", "Q", q_pre)
    p.wait("tensor", "Q", q_pre)
    p.run("gpsimd", lambda e: e.memset(ONESB[:, :], 1.0))
    p.run("gpsimd", lambda e: e.tensor_copy(IDENTF[:, :], IDENT[:, :]))
    p.run("gpsimd", lambda e: e.memset(ONES1[:, :], 1.0))
    p.run("gpsimd", lambda e: e.memset(MXW[:, :], 0.0))
    p.run("gpsimd", lambda e: e.memset(BIS[:, :], 0.0), inc="G")
    g_ones = p.snap("G")

    def layer_norm(pre_v):
        p.wait("vector", "V", pre_v)
        for c in range(NCH):
            p.run("vector", lambda e, c=c: e.tensor_copy(XLN[:, c * T:(c + 1) * T], RX[:, c * T:(c + 1) * T]),
                  inc="V" if c == NCH - 1 else None)
        v_cast = p.snap("V")
        v_norm_prev = [0]
        SQB = SQBF[:, :].bitcast(BF16)
        for tc in range(4):
            tsl = slice(tc * 512, (tc + 1) * 512)
            p.wait("scalar", "V", v_cast)
            for c in range(NCH):
                p.run("scalar", lambda e, c=c, tsl=tsl: e.activation(
                    SQB[:, c * 512:(c + 1) * 512], RX[:, c * T:(c + 1) * T][:, tsl], AF.Square),
                    inc="S" if c == NCH - 1 else None)
            s_sq = p.snap("S")
            p.wait("tensor", "V", v_cast)
            p.wait("tensor", "S", s_sq)
            p.wait("tensor", "G", g_ones)
            for c in range(NCH):
                p.run("tensor", lambda e, c=c, tsl=tsl: e.matmul(
                    PB[4][0:1, :], ONESB[:, 0:1], XLN[:, c * T:(c + 1) * T][:, tsl],
                    start=(c == 0), stop=(c == NCH - 1)))
            for c in range(NCH):
                p.run("tensor", lambda e, c=c: e.matmul(
                    PB[5][0:1, :], ONESB[:, 0:1], SQB[:, c * 512:(c + 1) * 512],
                    start=(c == 0), stop=(c == NCH - 1)),
                    inc="T" if c == NCH - 1 else None)
            t_sums = p.snap("T")
            p.wait("scalar", "T", t_sums)
            p.wait("scalar", "V", v_norm_prev[0])
            p.run("scalar", lambda e: e.activation(SCRB[0:1, 0:512], PB[4][0:1, :], AF.Copy))
            p.run("scalar", lambda e: e.activation(SCRB[0:1, 512:1024], PB[5][0:1, :], AF.Copy), inc="S")
            s_rows = p.snap("S")
            p.wait("tensor", "S", s_rows)
            p.run("tensor", lambda e: e.matmul(PB[2][:, :], ONES1[0:1, :], SCRB[0:1, 0:512], start=True, stop=True))
            p.run("tensor", lambda e: e.matmul(PB[3][:, :], ONES1[0:1, :], SCRB[0:1, 512:1024], start=True, stop=True), inc="T")
            t_bc = p.snap("T")
            p.wait("vector", "T", t_bc)
            p.run("vector", lambda e: e.tensor_scalar(SCRB[:, 0:512], PB[2][:, :], 1.0 / 1024.0, None, ALU.mult))
            p.run("vector", lambda e: e.tensor_mul(SCRB[:, 512:1024], SCRB[:, 0:512], SCRB[:, 0:512]))
            p.run("vector", lambda e: e.scalar_tensor_tensor(
                SCRB[:, 1024:1536], PB[3][:, :], 1.0 / 1024.0, SCRB[:, 512:1024], ALU.mult, ALU.subtract), inc="V")
            v_var = p.snap("V")
            p.wait("scalar", "V", v_var)
            p.run("scalar", lambda e: e.activation(SCRB[:, 1536:2048], SCRB[:, 1024:1536], AF.Ln, bias=1e-5))
            p.run("scalar", lambda e: e.activation(SCRB[:, 1536:2048], SCRB[:, 1536:2048], AF.Exp, scale=-0.5), inc="S")
            s_rstd = p.snap("S")
            p.wait("vector", "S", s_rstd)
            for c in range(NCH):
                p.run("vector", lambda e, c=c, tsl=tsl: e.tensor_sub(
                    SCRA[:, 0:512], RX[:, c * T:(c + 1) * T][:, tsl], SCRB[:, 0:512]))
                p.run("vector", lambda e, c=c, tsl=tsl: e.tensor_mul(
                    XLN[:, c * T:(c + 1) * T][:, tsl], SCRA[:, 0:512], SCRB[:, 1536:2048]),
                    inc="V" if c == NCH - 1 else None)
            v_norm_prev[0] = p.snap("V")
        return p.snap("V")

    state = {"pt_free_v": 0, "last_exp_s": 0, "proj_evac_v": 0, "pxb_free_v": 0, "linv_used_v": 0}

    O_Q, O_K, O_VT, O_EXPS, O_CTX, O_WQT, O_WOT, O_OST = \
        0, 4096, 8192, 12512, 14560, 30944, 31968, 32992

    def phase1():
        p.wait("vector", "B", b_x)
        v_ln = layer_norm(0)
        p.wait("gpsimd", "G", g_ones)
        for pr_i in range(2):
            base = O_VT + pr_i * 2080
            p.run("gpsimd", lambda e, base=base: e.memset(
                bass.AP(BIGt, base + 64, [[39296, P], [130, 16]]), 1.0))
            p.run("gpsimd", lambda e, base=base: e.memset(
                bass.AP(BIGt, base + 129, [[39296, P], [130, 16]]), 1.0),
                inc="G" if pr_i == 1 else None)
        g_vt = p.snap("G")

        tile_plan = []
        for g in range(4):
            for part in range(3):
                for tcp in range(2):
                    for pr_i in range(2):
                        for kc in range(NCH):
                            tile_plan.append((kc, part * D + (2 * g + pr_i) * P))
        wq_marks = []
        wq_consumed = []

        def issue_wq(n):
            kc, col0 = tile_plan[n]
            if n >= NCH:
                p.wait("gpsimd", "T", wq_consumed[n - NCH])
            sname = "R%d" % (n % NCH)
            p.run("gpsimd", lambda e, kc=kc, col0=col0, n=n: e.dma_start(
                big(O_WQT + (n % NCH) * P, P), wqkv.ap()[kc * P:(kc + 1) * P, col0:col0 + P]), inc=sname)
            wq_marks.append((sname, p.snap(sname)))

        tile_i = [0]

        def proj_piece(g, part, tcp):
            p.wait("tensor", "V", v_ln)
            p.wait("tensor", "S", state["last_exp_s"])
            for pr_i in range(2):
                pr = 2 * g + pr_i
                p.wait("tensor", "V", state["proj_evac_v"])
                for kc in range(NCH):
                    n = tile_i[0]
                    while len(wq_marks) <= min(n + NCH - 1, len(tile_plan) - 1):
                        issue_wq(len(wq_marks))
                    p.wait("tensor", wq_marks[n][0], wq_marks[n][1])
                    for sub in range(2):
                        tc = 2 * tcp + sub
                        tsl = slice(tc * 512, (tc + 1) * 512)
                        p.run("tensor", lambda e, n=n, kc=kc, tsl=tsl, sub=sub: e.matmul(
                            PB[sub][:, :], big(O_WQT + (n % NCH) * P, P),
                            XLN[:, kc * T:(kc + 1) * T][:, tsl],
                            start=(kc == 0), stop=(kc == NCH - 1)),
                            inc="T" if sub == 1 else None)
                    wq_consumed.append(p.cnt["T"])
                    tile_i[0] += 1
                t_mm = p.cnt["T"]
                for sub in range(2):
                    tc = 2 * tcp + sub
                    tsl = slice(tc * 512, (tc + 1) * 512)
                    p.wait("vector", "T", t_mm)
                    if part == 0:
                        dst = big(O_Q + pr_i * T, T)[:, tsl]
                        p.run("vector", lambda e, dst=dst, sub=sub, pr=pr: e.tensor_scalar(
                            dst, PB[sub][:, :], BQKV[:, pr:pr + 1], None, ALU.add), inc="V")
                    elif part == 1:
                        dst = big(O_K + pr_i * T, T)[:, tsl]
                        p.run("vector", lambda e, dst=dst, sub=sub, pr=pr: e.tensor_scalar(
                            dst, PB[sub][:, :], BQKV[:, 8 + pr:9 + pr], 0.125, ALU.add, ALU.mult), inc="V")
                    else:
                        vbf = SQBF[:, :].bitcast(BF16)
                        p.run("vector", lambda e, sub=sub, pr=pr, tc=tc, vbf=vbf: e.tensor_scalar(
                            vbf[:, (tc % 2) * 512:(tc % 2) * 512 + 512], PB[sub][:, :],
                            BQKV[:, 16 + pr:17 + pr], None, ALU.add), inc="V")
                        v_vb = p.snap("V")
                        p.wait("tensor", "V", v_vb)
                        p.wait("tensor", "G", g_vt)
                        for j in range(4):
                            kt = tc * 4 + j
                            p.wait("tensor", "V", state["pt_free_v"])
                            p.run("tensor", lambda e, tc=tc, j=j, vbf=vbf: e.transpose(
                                PT[:, 0:128], vbf[:, (tc % 2) * 512 + j * 128:(tc % 2) * 512 + (j + 1) * 128],
                                IDENT[:, :]), inc="T")
                            t_tr = p.cnt["T"]
                            p.wait("vector", "T", t_tr)
                            dst = bass.AP(BIGt, O_VT + pr_i * 2080 + kt * 130, [[39296, P], [65, 2], [1, 64]])
                            p.run("vector", lambda e, dst=dst: e.tensor_copy(
                                dst, PT[:, 0:128].rearrange("p (a b) -> p a b", a=2)), inc="V")
                            state["pt_free_v"] = p.snap("V")
                state["proj_evac_v"] = p.snap("V")
            return p.snap("V")

        def attn_pair(g, pr_i, proj_v):
            pr = 2 * g + pr_i
            qb = big(O_Q + pr_i * T, T)
            kb = big(O_K + pr_i * T, T)
            vt0 = O_VT + pr_i * 2080
            ctx_base = O_CTX + pr * T
            p.wait("tensor", "V", proj_v)
            prev_exp = {0: 0, 1: 0}
            prev_av_t = 0
            v_prev_evac = 0
            for qq in range(4):
                q0, q1 = qq * 512, (qq + 1) * 512
                ktmax = (qq + 1) * 4
                p.wait("tensor", "V", v_prev_evac)
                st_info = {}

                def emit_st(kt, qb=qb, kb=kb):
                    lo = max(q0, kt * P)
                    span = q1 - lo
                    par = kt % 2
                    stb_e = PB[4] if par == 0 else PB[0]
                    stb_o = PB[5] if par == 0 else PB[1]
                    exps_o = O_EXPS + par * 1024
                    p.wait("tensor", "S", prev_exp[par])
                    p.run("tensor", lambda e, kt=kt, lo=lo, q1=q1, qb=qb, kb=kb, stb_e=stb_e: e.matmul(
                        stb_e[:, 0:q1 - lo], kb[0:64, kt * P:(kt + 1) * P], qb[0:64, lo:q1],
                        start=True, stop=True, tile_position=(0, 0)))
                    p.run("tensor", lambda e, kt=kt, lo=lo, q1=q1, qb=qb, kb=kb, stb_o=stb_o: e.matmul(
                        stb_o[:, 0:q1 - lo], kb[64:128, kt * P:(kt + 1) * P], qb[64:128, lo:q1],
                        start=True, stop=True, tile_position=(64, 0)), inc="T")
                    t_st = p.cnt["T"]
                    p.wait("scalar", "T", t_st)
                    p.wait("scalar", "T", prev_av_t)
                    p.run("scalar", lambda e, exps_o=exps_o, span=span, stb_e=stb_e: e.activation(
                        big(exps_o, 512)[:, 0:span], stb_e[:, 0:span], AF.Exp, bias=-SOFT_SHIFT))
                    p.run("scalar", lambda e, exps_o=exps_o, span=span, stb_o=stb_o: e.activation(
                        big(exps_o + 512, 512)[:, 0:span], stb_o[:, 0:span], AF.Exp, bias=-SOFT_SHIFT), inc="S")
                    s_exp = p.cnt["S"]
                    prev_exp[par] = s_exp
                    state["last_exp_s"] = s_exp
                    v_mask = 0
                    if lo == kt * P:
                        p.wait("vector", "S", s_exp)
                        p.run("vector", lambda e, exps_o=exps_o: e.tensor_mul(
                            big(exps_o, 512)[:, 0:128], big(exps_o, 512)[:, 0:128], DIAGM[:, :]))
                        p.run("vector", lambda e, exps_o=exps_o: e.tensor_mul(
                            big(exps_o + 512, 512)[:, 0:128], big(exps_o + 512, 512)[:, 0:128], DIAGM[:, :]),
                            inc="V")
                        v_mask = p.snap("V")
                    st_info[kt] = (s_exp, v_mask, lo, span, exps_o)

                def emit_av(kt, ktmax=ktmax, vt0=vt0):
                    s_exp, v_mask, lo, span, exps_o = st_info[kt]
                    p.wait("tensor", "S", s_exp)
                    p.wait("tensor", "V", v_mask)
                    ca = lo - q0
                    p.run("tensor", lambda e, vt0=vt0, kt=kt, exps_o=exps_o, ca=ca, span=span, st=(kt == 0), sp=(kt == ktmax - 1): e.matmul(
                        PB[2][0:65, ca:ca + span], big(vt0 + kt * 130, 65), big(exps_o, 512)[:, 0:span],
                        start=st, stop=sp, skip_group_check=True))
                    p.run("tensor", lambda e, vt0=vt0, kt=kt, exps_o=exps_o, ca=ca, span=span, st=(kt == 0), sp=(kt == ktmax - 1): e.matmul(
                        PB[3][0:65, ca:ca + span], big(vt0 + kt * 130 + 65, 65), big(exps_o + 512, 512)[:, 0:span],
                        start=st, stop=sp, skip_group_check=True), inc="T")

                emit_st(0)
                for kt in range(ktmax):
                    if kt + 1 < ktmax:
                        emit_st(kt + 1)
                    emit_av(kt)
                    prev_av_t = p.cnt["T"]
                t_av = p.cnt["T"]
                for head, bank in ((0, PB[2]), (1, PB[3])):
                    p.wait("scalar", "T", t_av)
                    p.run("scalar", lambda e, bank=bank: e.activation(
                        SCRB[0:1, 0:512], bank[64:65, :], AF.Ln), inc="S")
                    s_ln = p.cnt["S"]
                    p.wait("tensor", "S", s_ln)
                    p.wait("tensor", "V", state.get("pxb_free_v", 0) if isinstance(state, dict) else 0)
                    p.run("tensor", lambda e: e.matmul(PXB[:, :], ONES1[0:1, :], SCRB[0:1, 0:512],
                                                       start=True, stop=True), inc="T")
                    t_lb = p.cnt["T"]
                    p.wait("scalar", "T", t_lb)
                    p.wait("scalar", "V", state["linv_used_v"])
                    p.run("scalar", lambda e: e.activation(SCRA[0:64, 0:512], PXB[0:64, :], AF.Exp, scale=-1.0),
                          inc="S")
                    s_li = p.cnt["S"]
                    p.wait("vector", "S", s_li)
                    if head == 0:
                        dst = big(ctx_base, T)[0:64, q0:q1]
                        p.run("vector", lambda e, dst=dst, bank=bank: e.tensor_mul(
                            dst, bank[0:64, :], SCRA[0:64, 0:512]), inc="V")
                        state["linv_used_v"] = p.snap("V")
                    else:
                        p.run("vector", lambda e, bank=bank: e.tensor_mul(
                            big(O_OST, 512)[0:64, :], bank[0:64, :], SCRA[0:64, 0:512]), inc="V")
                        v_st = p.snap("V")
                        state["linv_used_v"] = v_st
                        p.wait("tensor", "V", v_st)
                        p.run("tensor", lambda e: e.matmul(
                            PXB[:, :], ISH[:, :], big(O_OST, 512)[0:64, :],
                            start=True, stop=True), inc="T")
                        t_sh = p.cnt["T"]
                        p.wait("vector", "T", t_sh)
                        dst = big(ctx_base, T)[64:128, q0:q1]
                        p.run("vector", lambda e, dst=dst: e.tensor_copy(
                            dst, PXB[64:128, :]), inc="V")
                        state["pxb_free_v"] = p.snap("V")
                v_prev_evac = p.snap("V")
            return p.snap("V")

        for _ in range(NCH):
            issue_wq(len(wq_marks))
        piece_order = [(part, tcp) for part in range(3) for tcp in range(2)]
        proj_done = {}
        for part, tcp in piece_order:
            proj_piece(0, part, tcp)
        proj_done[0] = p.snap("V")
        for g in range(4):
            for pr_i in range(2):
                attn_pair(g, pr_i, proj_done[g])
            if g < 3:
                for part, tcp in piece_order:
                    proj_piece(g + 1, part, tcp)
                proj_done[g + 1] = p.snap("V")

        ctx_v = p.snap("V")
        wo_marks = []
        wo_consumed = []

        def issue_wo(n):
            mc, kc = divmod(n, NCH)
            if n >= NCH:
                p.wait("gpsimd", "T", wo_consumed[n - NCH])
            else:
                p.wait("gpsimd", "T", p.cnt["T"])
            sname = "R%d" % (n % NCH)
            p.run("gpsimd", lambda e, mc=mc, kc=kc, n=n: e.dma_start(
                big(O_WOT + (n % NCH) * P, P), wout.ap()[kc * P:(kc + 1) * P, mc * P:(mc + 1) * P]), inc=sname)
            wo_marks.append((sname, p.snap(sname)))

        for _ in range(NCH):
            issue_wo(len(wo_marks))
        p.wait("tensor", "V", ctx_v)
        p.wait("tensor", "S", state["last_exp_s"])
        flv = [0, 0]
        for mc in range(NCH):
            while len(wo_marks) <= min((mc + 1) * NCH - 1, 8 * NCH - 1):
                issue_wo(len(wo_marks))
            for tc in range(4):
                tsl = slice(tc * 512, (tc + 1) * 512)
                i = mc * 4 + tc
                bank = PB[i % 2]
                p.wait("tensor", "V", flv[i % 2])
                for kc in range(NCH):
                    n = mc * NCH + kc
                    p.wait("tensor", wo_marks[n][0], wo_marks[n][1])
                    p.run("tensor", lambda e, n=n, kc=kc, tsl=tsl, bank=bank: e.matmul(
                        bank[:, :], big(O_WOT + (n % NCH) * P, P),
                        big(O_CTX + kc * T, T)[:, tsl], start=(kc == 0), stop=(kc == NCH - 1)),
                        inc="T" if (kc == NCH - 1 or tc == 3) else None)
                    if tc == 3:
                        wo_consumed.append(p.cnt["T"])
                t_op = p.cnt["T"]
                p.wait("vector", "T", t_op)
                dst = RX[:, mc * T:(mc + 1) * T][:, tsl]
                p.run("vector", lambda e, dst=dst, bank=bank, mc=mc: e.scalar_tensor_tensor(
                    dst, bank[:, :], BOUT[:, mc:mc + 1], dst, ALU.add, ALU.add), inc="V")
                flv[i % 2] = p.snap("V")
        return p.snap("V")

    O2_W1S, O2_W2S, O2_HT, O2_GW, O2_GATH, O2_ONES = 0, 16384, 32768, 33792, 34304, 36352

    def phase2(prev_v):
        LOGSB = SCRC
        GBC = SCRC
        for _eng in ("gpsimd", "sync", "scalar"):
            p.wait(_eng, "T", p.cnt["T"])
            p.wait(_eng, "V", p.cnt["V"])
            p.wait(_eng, "S", p.cnt["S"])
        v_ln = layer_norm(prev_v)
        for kc in range(NCH):
            p.run("gpsimd", lambda e, kc=kc: e.dma_start(
                big(O2_GW + kc * 32, NE), gwT.ap()[kc * P:(kc + 1) * P, :]), inc="W")
        w_gw = p.snap("W")
        p.wait("tensor", "W", w_gw)
        p.wait("tensor", "V", v_ln)
        v_logm = 0
        for tc in range(4):
            tsl = slice(tc * 512, (tc + 1) * 512)
            p.wait("tensor", "V", v_logm)
            for kc in range(NCH):
                p.run("tensor", lambda e, kc=kc, tsl=tsl: e.matmul(
                    PB[4][0:8, :], big(O2_GW + kc * 32, NE), XLN[:, kc * T:(kc + 1) * T][:, tsl],
                    start=(kc == 0), stop=(kc == NCH - 1)), inc="T" if kc == NCH - 1 else None)
            t_lg = p.cnt["T"]
            p.wait("vector", "T", t_lg)
            p.run("vector", lambda e, tsl=tsl: e.tensor_copy(LOGSB[0:8, tsl], PB[4][0:8, :]), inc="V")
            v_logm = p.snap("V")
        v_log = p.snap("V")
        p.wait("scalar", "V", v_log)
        p.run("scalar", lambda e: e.dma_start(ccg_in.ap().rearrange("(a b) -> a b", a=8), LOGSB[0:8, :]), inc="Q")
        q_st = p.snap("Q")
        p.wait("gpsimd", "Q", q_st)
        p.run("gpsimd", lambda e: e.collective_compute(
            "AllGather", ALU.bypass, RG, [ccg_in.ap()], [ccg_out.ap()]), inc="G")
        g_ag = p.snap("G")
        p.wait("gpsimd", "G", g_ag)
        p.run("gpsimd", lambda e: e.dma_start(
            bigf(O2_GATH, 1024), ccg_out.ap().rearrange("(a b) -> a b", a=P)), inc="cc0")
        w_gath = p.snap("cc0")

        w1_marks, w2_marks, q_consumed = [], [], {}

        def issue_quarter(eq):
            e_i, q_i = divmod(eq, 4)
            sl = eq % 2
            sname = "q%d" % sl
            if eq >= 2:
                p.wait("gpsimd", "T", q_consumed[eq - 2])
            for kc in range(NCH):
                p.run("gpsimd", lambda e, e_i=e_i, q_i=q_i, kc=kc, sl=sl: e.dma_start(
                    big(O2_W1S + sl * 8192 + kc * 1024, 1024),
                    w1T.ap()[e_i][kc * P:(kc + 1) * P, q_i * 1024:(q_i + 1) * 1024]), inc=sname)
            for hc in range(NCH):
                p.run("gpsimd", lambda e, e_i=e_i, q_i=q_i, hc=hc, sl=sl: e.dma_start(
                    big(O2_W2S + sl * 8192 + hc * 1024, 1024),
                    w2T.ap()[e_i][q_i * 1024 + hc * P:q_i * 1024 + (hc + 1) * P, :]), inc=sname)
            w1_marks.append((sname, p.snap(sname)))
            w2_marks.append((sname, p.snap(sname)))

        issue_quarter(0)
        issue_quarter(1)

        p.scope = "p2bis"
        p.wait("gpsimd", "cc0", w_gath)
        p.run("gpsimd", lambda e: e.memset(BIS[:, 0:1], -64.0))
        p.run("gpsimd", lambda e: e.memset(BIS[:, 1:2], 64.0))
        p.run("gpsimd", lambda e: e.memset(bigf(O2_ONES, 128), 1.0), inc="G")
        g_binit = p.snap("G")
        p.wait("vector", "G", g_binit)
        p.wait("vector", "cc0", w_gath)
        p.wait("tensor", "V", state.get("pxb_free_v", 0))
        for it in range(30):
            p.run("vector", lambda e: e.tensor_add(BIS[:, 2:3], BIS[:, 0:1], BIS[:, 1:2]))
            p.run("vector", lambda e: e.tensor_scalar(BIS[:, 2:3], BIS[:, 2:3], 0.5, None, ALU.mult))
            p.run("vector", lambda e: e.tensor_scalar(SCRA[:, 0:1024], bigf(O2_GATH, 1024), BIS[:, 2:3], None, ALU.is_lt))
            p.run("vector", lambda e: e.reduce_sum(BIS[:, 3:4], SCRA[:, 0:1024], AX.X), inc="V")
            v_c = p.snap("V")
            p.wait("tensor", "V", v_c)
            p.run("tensor", lambda e: e.matmul(PXB[:, 0:1], bigf(O2_ONES, 128), BIS[:, 3:4],
                                               start=True, stop=True), inc="T")
            t_tot = p.cnt["T"]
            p.wait("vector", "T", t_tot)
            p.run("vector", lambda e: e.tensor_copy(BIS[:, 4:5], PXB[:, 0:1]))
            p.run("vector", lambda e: e.tensor_scalar(BIS[:, 5:6], BIS[:, 4:5], float(CAP_K), None, ALU.is_lt))
            p.run("vector", lambda e: e.tensor_sub(BIS[:, 6:7], BIS[:, 2:3], BIS[:, 0:1]))
            p.run("vector", lambda e: e.tensor_mul(BIS[:, 6:7], BIS[:, 6:7], BIS[:, 5:6]))
            p.run("vector", lambda e: e.tensor_add(BIS[:, 0:1], BIS[:, 0:1], BIS[:, 6:7]))
            p.run("vector", lambda e: e.tensor_sub(BIS[:, 7:8], BIS[:, 2:3], BIS[:, 1:2]))
            p.run("vector", lambda e: e.tensor_mul(BIS[:, 8:9], BIS[:, 7:8], BIS[:, 5:6]))
            p.run("vector", lambda e: e.tensor_add(BIS[:, 1:2], BIS[:, 1:2], BIS[:, 7:8]))
            p.run("vector", lambda e: e.tensor_sub(BIS[:, 1:2], BIS[:, 1:2], BIS[:, 8:9]), inc="V")
        v_bis = p.snap("V")
        state["pxb_free_v"] = v_bis
        p.run("vector", lambda e: e.tensor_scalar(SCRA[0:8, 0:T], LOGSB[0:8, :], BIS[0:8, 0:1], None, ALU.is_ge))
        p.run("vector", lambda e: e.tensor_mul(LOGSB[0:8, :], LOGSB[0:8, :], SCRA[0:8, 0:T]), inc="V")
        v_g = p.snap("V")
        p.wait("scalar", "V", v_g)
        p.run("scalar", lambda e: e.dma_start(g_hbm.ap(), LOGSB[0:8, :]), inc="Q")
        q_g = p.snap("Q")

        gbc_marks = {}
        flush_done = {}

        def issue_gbc(e_i):
            p.wait("sync", "Q", q_g)
            p.wait("sync", "V", flush_done.get(e_i - 1, 0))
            p.run("sync", lambda e, e_i=e_i: e.dma_start(
                GBC[:, :], bass.AP(g_hbm, e_i * T, [[0, P], [1, T]])), inc="gb0")
            gbc_marks[e_i] = p.snap("gb0")

        p.scope = "p2exp"
        issue_gbc(0)
        p.wait("tensor", "V", v_log)
        gelu_done = {}
        ymm_done = {}
        flush_y_v = [0]
        for e_i in range(NE):
            if e_i >= 1:
                issue_gbc(e_i)
            for q_i in range(4):
                eq = e_i * 4 + q_i
                sl = eq % 2
                p.wait("tensor", w1_marks[eq][0], w1_marks[eq][1])
                for tb in range(NCH):
                    tb_sl = slice(tb * 256, (tb + 1) * 256)

                    t_h_of = {}

                    def emit_hmm(hcp, sl=sl, tb_sl=tb_sl, tb=tb, eq=eq):
                        par = hcp % 2
                        hbank = PB[4 + par]
                        hidx = eq * 32 + tb * 4 + hcp
                        p.wait("tensor", "S", gelu_done.get(hidx - 2, 0))
                        for sub in range(2):
                            hc = 2 * hcp + sub
                            for kc in range(NCH):
                                p.run("tensor", lambda e, sl=sl, kc=kc, hc=hc, tb_sl=tb_sl, hbank=hbank, sub=sub: e.matmul(
                                    hbank[:, sub * 256:(sub + 1) * 256],
                                    big(O2_W1S + sl * 8192 + kc * 1024 + hc * P, P),
                                    XLN[:, kc * T:(kc + 1) * T][:, tb_sl],
                                    start=(kc == 0), stop=(kc == NCH - 1)),
                                    inc="T" if (kc == NCH - 1 and sub == 1) else None)
                        t_h_of[hcp] = p.cnt["T"]

                    def emit_gelu(hcp, tb=tb, eq=eq):
                        par = hcp % 2
                        hbank = PB[4 + par]
                        hidx = eq * 32 + tb * 4 + hcp
                        p.wait("scalar", "T", t_h_of[hcp])
                        p.wait("scalar", "T", ymm_done.get(hidx - 2, 0))
                        p.run("scalar", lambda e, hbank=hbank, par=par: e.activation(
                            big(O2_HT + par * 512, 512), hbank[:, :], AF.Gelu_apprx_tanh), inc="S")
                        gelu_done[hidx] = p.cnt["S"]

                    def emit_y(hcp, sl=sl, tb=tb, eq=eq):
                        par = hcp % 2
                        hidx = eq * 32 + tb * 4 + hcp
                        p.wait("tensor", "S", gelu_done[hidx])
                        if hcp == 0:
                            p.wait("tensor", "V", flush_y_v[0])
                        for sub in range(2):
                            hc = 2 * hcp + sub
                            for dd in range(NCH):
                                ybank = PB[dd // 2]
                                ycol = (dd % 2) * 256
                                p.run("tensor", lambda e, sl=sl, hc=hc, dd=dd, ybank=ybank, ycol=ycol, par=par, sub=sub,
                                      st=(hcp == 0 and sub == 0 and dd % 2 == 0),
                                      sp=(hcp == 3 and sub == 1 and dd % 2 == 1): e.matmul(
                                    ybank[:, ycol:ycol + 256],
                                    big(O2_W2S + sl * 8192 + hc * 1024 + dd * P, P),
                                    big(O2_HT + par * 512 + sub * 256, 256),
                                    start=st, stop=sp, skip_group_check=True),
                                    inc="T" if (sub == 1 and dd == NCH - 1) else None)
                        ymm_done[hidx] = p.cnt["T"]

                    emit_hmm(0)
                    emit_gelu(0)
                    emit_hmm(1)
                    emit_gelu(1)
                    emit_hmm(2)
                    emit_y(0)
                    emit_gelu(2)
                    emit_hmm(3)
                    emit_y(1)
                    emit_gelu(3)
                    emit_y(2)
                    emit_y(3)
                    t_y = p.cnt["T"]
                    p.wait("vector", "T", t_y)
                    p.wait("vector", "gb0", gbc_marks[e_i])
                    for j in range(4):
                        gsrc = bass.AP(SCRC, tb * 256, [[T, P], [0, 2], [1, 256]])
                        p.run("vector", lambda e, j=j, gsrc=gsrc: e.tensor_mul(
                            SCRA[:, 0:512], PB[j][:, :], gsrc))
                        rxap = bass.AP(RX, (2 * j) * T + tb * 256, [[NCH * T, P], [T, 2], [1, 256]])
                        p.run("vector", lambda e, rxap=rxap: e.tensor_add(rxap, rxap, SCRA[:, 0:512]),
                              inc="V" if j == 3 else None)
                    flush_y_v[0] = p.snap("V")
                    if q_i == 3 and tb == NCH - 1:
                        flush_done[e_i] = p.snap("V")
                q_consumed[eq] = p.cnt["T"]
                if eq + 2 < 32:
                    issue_quarter(eq + 2)
        return p.snap("V")

    O3_QK, O3_Q, O3_K, O3_VTOK, O3_WVB, O3_WQKB, O3_EX, O3_CTX3, O3_OHS, O3_SROW = \
        0, 2048, 4096, 6144, 22528, 24576, 25600, 27648, 29696, 32256

    def phase3(prev_v):
        for _eng in ("gpsimd", "sync", "scalar"):
            p.wait(_eng, "T", p.cnt["T"])
            p.wait(_eng, "V", p.cnt["V"])
            p.wait(_eng, "S", p.cnt["S"])
        p.scope = "p3ln"
        v_ln = layer_norm(prev_v)
        p.scope = "p3s2d"
        p.run("scalar", lambda e: e.dma_start(bigf(O3_OHS, 1280)[0:L1_NB, :], oh_c.ap()), inc="Q")
        p.run("scalar", lambda e: e.dma_start(bigf(O3_SROW, 1280)[0:1, :], maskadd_c.ap()), inc="Q")
        q_oh = p.snap("Q")
        for kc in range(NCH):
            p.run("gpsimd", lambda e, kc=kc: e.dma_start(
                big(O3_WQKB + kc * P, P), wqk.ap()[kc * P:(kc + 1) * P, :]), inc="W")
        w_qk = p.snap("W")
        p.wait("vector", "Q", q_pre)
        p.run("vector", lambda e: e.tensor_scalar(GAMT[:, 2:3], GAMT[:, 1:2], 1.0 / 32.0, None, ALU.mult), inc="V")
        v_gam = p.snap("V")
        p.run("vector", lambda e: e.tensor_scalar(LROW[:, 1:2], LROW[:, 0:1], math.sqrt(P), None, ALU.mult), inc="V")
        v_rel = p.snap("V")
        p.wait("tensor", "V", v_rel)
        p.wait("tensor", "Q", q_oh)
        for ch in range(3):
            a = ch * 512
            b = min(a + 512, 1280)
            p.wait("tensor", "V", p.cnt["V"] if ch > 0 else 0)
            p.run("tensor", lambda e, a=a, b=b: e.matmul(
                PXB[0:1, 0:b - a], LROW[:, 1:2], bigf(O3_OHS, 1280)[0:L1_NB, a:b],
                start=True, stop=True), inc="T")
            t_mm = p.cnt["T"]
            p.wait("vector", "T", t_mm)
            p.run("vector", lambda e, a=a, b=b: e.tensor_add(
                bigf(O3_SROW, 1280)[0:1, a:b], bigf(O3_SROW, 1280)[0:1, a:b], PXB[0:1, 0:b - a]), inc="V")
        v_srow = p.snap("V")
        p.wait("scalar", "V", v_srow)
        p.run("scalar", lambda e: e.dma_start(sext_hbm.ap()[None, :], bigf(O3_SROW, 1280)[0:1, :]), inc="Q")
        q_sx = p.snap("Q")
        p.wait("scalar", "Q", q_sx)
        for pp in range(P):
            p.run("scalar", lambda e, pp=pp: e.dma_start(
                SQBF[pp:pp + 1, 0:1024], bass.AP(sext_hbm, 127 - pp, [[1, 1], [1, 1024]])), inc="Q")
        q_s2d = p.snap("Q")

        p.scope = "p3qk"
        p.wait("tensor", "W", w_qk)
        p.wait("tensor", "V", v_ln)
        s_qkl = [0, 0]
        for tc in range(4):
            tsl = slice(tc * 512, (tc + 1) * 512)
            p.wait("tensor", "S", s_qkl[tc % 2])
            for kc in range(NCH):
                p.run("tensor", lambda e, kc=kc, tsl=tsl, tc=tc: e.matmul(
                    PB[4 + tc % 2][:, :], big(O3_WQKB + kc * P, P), XLN[:, kc * T:(kc + 1) * T][:, tsl],
                    start=(kc == 0), stop=(kc == NCH - 1)), inc="T" if kc == NCH - 1 else None)
            t_qk = p.cnt["T"]
            p.wait("scalar", "T", t_qk)
            p.run("scalar", lambda e, tsl=tsl, tc=tc: e.activation(
                big(O3_QK, 2048)[:, tsl], PB[4 + tc % 2][:, :], AF.Silu), inc="S")
            s_qkl[tc % 2] = p.snap("S")
        s_qk = p.snap("S")
        p.wait("vector", "S", s_qk)
        p.wait("vector", "V", v_gam)
        p.run("vector", lambda e: e.tensor_scalar(big(O3_Q, 2048), big(O3_QK, 2048), GAMT[:, 0:1], None, ALU.mult))
        p.run("vector", lambda e: e.tensor_scalar(big(O3_K, 2048), big(O3_QK, 2048), GAMT[:, 2:3], None, ALU.mult), inc="V")
        v_qkb = p.snap("V")

        p.scope = "p3vt"
        wv_marks = []
        wv_cons = []

        def issue_wv(n):
            kc = n % NCH
            if n >= 2:
                p.wait("gpsimd", "T", wv_cons[n - 2])
            sname = "wv%d" % (n % 2)
            p.run("gpsimd", lambda e, kc=kc, n=n: e.dma_start(
                big(O3_WVB + (n % 2) * 1024, 1024), wv.ap()[kc * P:(kc + 1) * P, :]), inc=sname)
            wv_marks.append((sname, p.snap(sname)))

        it_groups = [list(range(a, min(a + 3, 16))) for a in range(0, 16, 3)]
        s_vt = s_qk
        n_wv_total = len(it_groups) * NCH
        issue_wv(0)
        issue_wv(1)
        for gidx, grp in enumerate(it_groups):
            base_n = gidx * NCH
            for kci in range(NCH):
                kc = kci
                n = base_n + kci
                while len(wv_marks) <= min(n + 1, n_wv_total - 1):
                    issue_wv(len(wv_marks))
                p.wait("tensor", wv_marks[n][0], wv_marks[n][1])
                if kci == 0:
                    p.wait("tensor", "S", s_vt)
                    p.wait("tensor", "V", v_ln)
                for gi, it in enumerate(grp):
                    for vc in range(2):
                        p.run("tensor", lambda e, n=n, kc=kc, it=it, vc=vc, gi=gi: e.matmul(
                            PB[gi * 2 + vc][:, :], XLN[:, kc * T:(kc + 1) * T][:, it * P:(it + 1) * P],
                            big(O3_WVB + (n % 2) * 1024 + vc * 512, 512),
                            start=(kc == 0), stop=(kc == NCH - 1)),
                            inc="T" if (gi == len(grp) - 1 and vc == 1) else None)
                wv_cons.append(p.cnt["T"])
            t_v = p.cnt["T"]
            p.wait("scalar", "T", t_v)
            for gi, it in enumerate(grp):
                for vc in range(2):
                    p.run("scalar", lambda e, it=it, vc=vc, gi=gi: e.activation(
                        big(O3_VTOK + it * 1024 + vc * 512, 512), PB[gi * 2 + vc][:, :], AF.Silu),
                        inc="S" if (gi == len(grp) - 1 and vc == 1) else None)
            s_vt = p.snap("S")

        p.scope = "p3att"
        p.wait("vector", "Q", q_s2d)
        v_ctx = 0
        for ck in range(4):
            c0 = ck * 512
            s_ex = {}
            for jt in range(4):
                span = 512 - jt * P
                sb = PB[4 + jt % 2]
                p.wait("tensor", "S", s_ex.get(jt - 2, 0))
                p.wait("tensor", "V", v_qkb)
                p.wait("tensor", "S", s_vt)
                p.run("tensor", lambda e, c0=c0, jt=jt, span=span, sb=sb: e.matmul(
                    sb[:, 0:span], big(O3_K, 2048)[:, c0 + jt * P:c0 + (jt + 1) * P],
                    big(O3_Q, 2048)[:, c0 + jt * P:c0 + 512], start=True, stop=True), inc="T")
                t_s = p.cnt["T"]
                p.wait("vector", "T", t_s)
                p.run("vector", lambda e, span=span, sb=sb: e.tensor_add(
                    sb[:, 0:span], sb[:, 0:span], SQBF[:, 384:384 + span]), inc="V")
                v_b = p.snap("V")
                p.wait("scalar", "V", v_b)
                p.run("scalar", lambda e, jt=jt, span=span, sb=sb: e.activation(
                    big(O3_EX + jt * 512, 512)[:, 0:span], sb[:, 0:span], AF.Exp, bias=-L1_SHIFT), inc="S")
                s_ex[jt] = p.cnt["S"]
            for it in range(4):
                p.wait("tensor", "S", s_ex[it])
                p.wait("tensor", "V", v_ctx)
                for jt in range(it + 1):
                    lh = big(O3_EX + jt * 512, 512)[:, (it - jt) * P:(it - jt + 1) * P]
                    vt = big(O3_VTOK + (ck * 4 + jt) * 1024, 1024)
                    for vc in range(2):
                        p.run("tensor", lambda e, lh=lh, vt=vt, vc=vc, st=(jt == 0), sp=(jt == it): e.matmul(
                            PB[vc][:, :], lh, vt[:, vc * 512:(vc + 1) * 512],
                            start=st, stop=sp, skip_group_check=True))
                    p.run("tensor", lambda e, lh=lh, it=it, st=(jt == 0), sp=(jt == it): e.matmul(
                        PB[2][:, it:it + 1], lh, ONESB[:, 0:1],
                        start=st, stop=sp, skip_group_check=True),
                        inc="T" if jt == it else None)
                t_avx = p.cnt["T"]
                p.wait("vector", "T", t_avx)
                p.run("vector", lambda e, it=it: e.reciprocal(BIS[:, 8:9], PB[2][:, it:it + 1]))
                if dbg_t is not None:
                    p.run("vector", lambda e, ck=ck, it=it: e.tensor_copy(DQS[:, ck * 4 + it:ck * 4 + it + 1], PB[2][:, it:it + 1]))
                    p.run("vector", lambda e, ck=ck, it=it: e.tensor_copy(DQS[:, 32 + ck * 4 + it:32 + ck * 4 + it + 1], BIS[:, 8:9]))
                p.run("vector", lambda e: e.tensor_scalar(big(O3_CTX3, 1024)[:, 0:512], PB[0][:, :], BIS[:, 8:9], None, ALU.mult))
                p.run("vector", lambda e: e.tensor_scalar(big(O3_CTX3, 1024)[:, 512:1024], PB[1][:, :], BIS[:, 8:9], None, ALU.mult), inc="V")
                v_cs = p.snap("V")
                for c in range(NCH):
                    p.wait("tensor", "V", v_cs)
                    p.wait("tensor", "V", state["pt_free_v"])
                    p.run("tensor", lambda e, c=c: e.transpose(
                        PT[:, 0:128], big(O3_CTX3, 1024)[:, c * P:(c + 1) * P], IDENT[:, :]), inc="T")
                    t_tr = p.cnt["T"]
                    p.wait("vector", "T", t_tr)
                    rxs = RX[:, c * T:(c + 1) * T][:, c0 + it * P:c0 + (it + 1) * P]
                    p.run("vector", lambda e, rxs=rxs: e.tensor_add(rxs, rxs, PT[:, 0:128]), inc="V")
                    state["pt_free_v"] = p.snap("V")
                if dbg_t is not None:
                    p.wait("gpsimd", "V", v_cs)
                    p.run("gpsimd", lambda e, ck=ck, it=it: e.dma_start(
                        dbg_t.ap()[ck * 4 + it], big(O3_CTX3, 1024)), inc="cc0")
                    p.wait("gpsimd", "cc0", p.cnt["cc0"])
                    p.wait("vector", "cc0", p.cnt["cc0"])
                v_ctx = p.snap("V")
        return p.snap("V")

    O4_OB, O4_NW, O4_TR = 0, 32768, 36864

    def ob_chunk(ic):
        if ic < 8:
            return XLN[:, ic * T:(ic + 1) * T]
        return big(O4_OB + (ic - 8) * T, T)

    def phase4_real(prev_v):
        for _eng in ("gpsimd", "sync", "scalar"):
            p.wait(_eng, "T", p.cnt["T"])
            p.wait(_eng, "V", p.cnt["V"])
            p.wait(_eng, "S", p.cnt["S"])
        p.scope = "p4ln"
        v_ln = layer_norm(prev_v)
        p.scope = "p4mx"
        p.wait("vector", "V", v_ln)
        for c in range(NCH):
            p.run("vector", lambda e, c=c: e.reduce_max(MXW[:, c:c + 1], XLN[:, c * T:(c + 1) * T], AX.X,
                                                        apply_absolute_value=True))
        p.run("vector", lambda e: e.reduce_max(MXW[:, 8:9], MXW[:, 0:8], AX.X), inc="V")
        v_mx = p.snap("V")

        def cross_max(col, v_wait):
            p.wait("gpsimd", "V", v_wait)
            p.run("gpsimd", lambda e, col=col: e.dma_start(SCRB[0:1, 0:128], MXW[:, col:col + 1]), inc="cc0")
            mm = p.snap("cc0")
            p.wait("vector", "cc0", mm)
            p.run("vector", lambda e, col=col: e.reduce_max(MXW[0:1, col:col + 1], SCRB[0:1, 0:128], AX.X), inc="V")
            return p.snap("V")

        g_mx = cross_max(8, v_mx)

        def armax(idx, src_cols, wait_kind, wait_val):
            p.wait("vector", wait_kind, wait_val)
            for i in range(8):
                if i < len(src_cols):
                    c = src_cols[i]
                    p.run("vector", lambda e, i=i, c=c: e.tensor_copy(MXW[0:1, 16 + i:17 + i], MXW[0:1, c:c + 1]),
                          inc="V" if i == len(src_cols) - 1 else None)
                else:
                    break
            nz = len(src_cols)
            if nz < 8:
                p.run("vector", lambda e, nz=nz: e.tensor_scalar(
                    MXW[0:1, 16 + nz:24], MXW[0:1, 16 + nz:24], 0.0, None, ALU.mult), inc="V")
            v_stg = p.snap("V")
            p.wait("scalar", "V", v_stg)
            p.run("scalar", lambda e, idx=idx: e.dma_start(
                ccm_in[idx].ap().rearrange("(a b) -> a b", a=1)[:, 0:8], MXW[0:1, 16:24]), inc="Q")
            p.run("scalar", lambda e, idx=idx: e.dma_start(
                ccm_in[idx].ap().rearrange("(a b) -> a b", a=1)[:, 8:16], MXW[0:1, 16:24]), inc="Q")
            q_c = p.snap("Q")
            p.wait("gpsimd", "Q", q_c)
            p.run("gpsimd", lambda e, idx=idx: e.collective_compute(
                "AllReduce", ALU.max, RG, [ccm_in[idx].ap()], [ccm_out[idx].ap()]), inc="G")
            g_c = p.snap("G")
            p.wait("gpsimd", "G", g_c)
            p.run("gpsimd", lambda e, idx=idx: e.dma_start(
                MXW[:, 24:32], bass.AP(ccm_out[idx], 0, [[0, P], [1, 8]])), inc="cc0")
            return p.snap("cc0")

        w_ar0 = armax(0, [8], "V", g_mx)
        p.wait("vector", "cc0", w_ar0)
        p.run("vector", lambda e: e.reciprocal(MXW[:, 9:10], MXW[:, 24:25]))
        p.run("vector", lambda e: e.tensor_scalar(MXW[:, 10:11], MXW[:, 9:10], 32767.0, None, ALU.mult))
        p.run("vector", lambda e: e.tensor_scalar(MXW[:, 11:12], MXW[:, 24:25], 1.0 / 32767.0, None, ALU.mult), inc="V")
        v_r0 = p.snap("V")

        SI32 = SCRB[:, :].bitcast(I32)
        p.wait("vector", "V", v_r0)
        for c in range(NCH):
            p.run("vector", lambda e, c=c: e.tensor_scalar(
                SCRA[:, :], XLN[:, c * T:(c + 1) * T], MXW[:, 10:11], 32768.0, ALU.mult, ALU.add),
                inc="V")
            p.wait("scalar", "V", p.snap("V"))
            p.run("scalar", lambda e: e.activation(SI32[:, :], SCRA[:, :], AF.Copy))
            p.run("scalar", lambda e: e.activation(SQBF[:, :], SI32[:, :], AF.Copy), inc="S")
            p.wait("vector", "S", p.snap("S"))
            p.run("vector", lambda e: e.scalar_tensor_tensor(
                SCRB[:, :], SCRA[:, :], 0.5, SQBF[:, :], ALU.subtract, ALU.is_ge))
            p.run("vector", lambda e: e.tensor_tensor(SQBF[:, :], SQBF[:, :], SCRB[:, :], ALU.add))
            p.run("vector", lambda e, c=c: e.tensor_scalar(
                XLN[:, c * T:(c + 1) * T], SQBF[:, :], -32768.0, MXW[:, 11:12], ALU.add, ALU.mult),
                inc="V" if c == NCH - 1 else None)
        v_xq = p.snap("V")

        p.scope = "p4a"
        ON_NAT, ON_STG, ON_TR = 0, 4096, 8192
        nat_marks = []
        nat_cons = []
        nat_plan = []

        def issue_nat(n):
            h, r0_, cols = nat_plan[n]
            if n >= 2:
                p.wait("sync", "T", nat_cons[n - 2])
            sname = "na%d" % (n % 2)
            p.run("sync", lambda e, h=h, r0_=r0_, cols=cols, n=n: e.dma_start(
                bigf(ON_NAT + (n % 2) * 2048, 1024)[:, 0:cols], h.ap()[r0_:r0_ + P, 0:cols]), inc=sname)
            nat_marks.append((sname, p.snap(sname)))

        def dequant_transpose(n, cols, n_tiles, tr_slot):
            nat = bigf(ON_NAT + (n % 2) * 2048, 1024)[:, 0:cols]
            nb = cols // 32
            p.wait("vector", nat_marks[n][0], nat_marks[n][1])
            p.run("vector", lambda e, nat=nat, nb=nb: e.tensor_reduce(
                DQS[:, 0:nb], nat.rearrange("p (b i) -> p b i", i=32), AX.X, ALU.max,
                apply_absolute_value=True))
            p.run("vector", lambda e, nb=nb: e.reciprocal(DQS[:, 32:32 + nb], DQS[:, 0:nb]))
            p.run("vector", lambda e, nb=nb: e.tensor_scalar(DQS[:, 32:32 + nb], DQS[:, 32:32 + nb], 7.0, None, ALU.mult))
            p.run("vector", lambda e, nb=nb: e.tensor_scalar(DQS[:, 64:64 + nb], DQS[:, 0:nb], 1.0 / 7.0, None, ALU.mult))
            rinv_bc = bass.AP(DQS, 32, [[96, P], [1, nb], [0, 32]])
            s_bc = bass.AP(DQS, 64, [[96, P], [1, nb], [0, 32]])
            p.run("vector", lambda e, nat=nat, rinv_bc=rinv_bc, nb=nb: e.tensor_mul(
                SCRA[:, 0:nb * 32], nat, rinv_bc), inc="V")
            v_m = p.snap("V")
            p.wait("scalar", "V", v_m)
            p.run("scalar", lambda e, nb=nb: e.add(SI32[:, 0:nb * 32], SCRA[:, 0:nb * 32], 8.0))
            p.run("scalar", lambda e, nb=nb: e.activation(SCRC[:, 0:nb * 32], SI32[:, 0:nb * 32], AF.Copy, bias=-8.0),
                  inc="S")
            s_rt = p.snap("S")
            p.wait("vector", "S", s_rt)
            p.run("vector", lambda e, nb=nb: e.scalar_tensor_tensor(
                SCRB[:, 0:nb * 32], SCRA[:, 0:nb * 32], 0.5, SCRC[:, 0:nb * 32], ALU.subtract, ALU.is_ge))
            p.run("vector", lambda e, nb=nb: e.tensor_tensor(
                SCRC[:, 0:nb * 32], SCRC[:, 0:nb * 32], SCRB[:, 0:nb * 32], ALU.add))
            p.run("vector", lambda e, s_bc=s_bc, nb=nb: e.tensor_mul(
                SQBF[:, 0:nb * 32].bitcast(BF16)[:, 0:nb * 32], SCRC[:, 0:nb * 32], s_bc), inc="V")
            v_dq = p.snap("V")
            deq = SQBF[:, 0:cols].bitcast(BF16)[:, 0:cols]
            t_last = 0
            for j in range(n_tiles):
                p.wait("tensor", "V", v_dq)
                p.wait("tensor", "V", state["pt_free_v"])
                p.run("tensor", lambda e, deq=deq, j=j: e.transpose(
                    PT[:, 0:128], deq[:, j * P:(j + 1) * P], IDENT[:, :]), inc="T")
                t_tr = p.cnt["T"]
                p.wait("vector", "T", t_tr)
                p.run("vector", lambda e, j=j, tr_slot=tr_slot: e.tensor_copy(
                    big(ON_TR + tr_slot * 1024 + j * P, P), PT[:, 0:128]), inc="V")
                state["pt_free_v"] = p.snap("V")
            return p.snap("V")

        for h, _hb in ((wup_n, up_hbm), (wgate_n, gate_hbm)):
            for ic in range(24):
                nat_plan.append((h, ic * P, 1024))
        issue_nat(0)
        issue_nat(1)
        stg_v = [0, 0]
        stg_b = [0, 0]
        s_bank = [0, 0]
        for wi, (h, hbm_dst, mxcol) in enumerate(((wup_n, up_hbm, 12), (wgate_n, gate_hbm, 13))):
            for ic in range(24):
                n = wi * 24 + ic
                while len(nat_marks) <= min(n + 1, len(nat_plan) - 1):
                    issue_nat(len(nat_marks))
                tr_slot = n % 2
                v_tr = dequant_transpose(n, 1024, NCH, tr_slot)
                if _CONFIG.get("dbg4") and n == 0:
                    BGF = BIGt[:, :].bitcast(F32)
                    p.wait("sync", nat_marks[0][0], nat_marks[0][1])
                    p.wait("sync", "V", v_tr)
                    p.run("sync", lambda e: e.dma_start(dbg_t.ap()[2], bigf(ON_NAT, 1024)), inc="dbx")
                    p.wait("sync", "dbx", p.snap("dbx"))
                    p.run("sync", lambda e, BGF=BGF: e.dma_start(
                        dbg_t.ap()[0][:, 0:512], BGF[:, ON_TR // 2:ON_TR // 2 + 512]), inc="dbx")
                    p.wait("sync", "dbx", p.snap("dbx"))
                    p.run("sync", lambda e: e.dma_start(dbg_t.ap()[4][:, 0:96], DQS[:, :]), inc="dbx")
                    p.wait("sync", "dbx", p.snap("dbx"))
                    p.wait("sync", "V", v_xq)
                    p.run("sync", lambda e: e.dma_start(
                        dbg_t.ap()[3][:, 0:1024], XLNt[:, :].bitcast(F32)[:, 0:1024]), inc="dbx")
                    g_d = p.snap("dbx")
                    p.wait("sync", "dbx", g_d)
                    p.wait("vector", "dbx", g_d)
                p.wait("tensor", "V", v_tr)
                p.wait("tensor", "V", v_xq)
                sslot = n % 2
                p.wait("scalar", "st%d" % sslot, stg_b[sslot])
                for tc in range(4):
                    tsl = slice(tc * 512, (tc + 1) * 512)
                    p.wait("tensor", "S", s_bank[tc % 2])
                    for kc in range(NCH):
                        p.run("tensor", lambda e, kc=kc, tsl=tsl, tc=tc, tr_slot=tr_slot: e.matmul(
                            PB[tc % 2][:, :], big(ON_TR + tr_slot * 1024 + kc * P, P),
                            XLN[:, kc * T:(kc + 1) * T][:, tsl],
                            start=(kc == 0), stop=(kc == NCH - 1)), inc="T" if kc == NCH - 1 else None)
                    t_up = p.cnt["T"]
                    nat_cons.append(t_up) if tc == 3 else None
                    p.wait("scalar", "T", t_up)
                    p.run("scalar", lambda e, tsl=tsl, sslot=sslot, tc=tc: e.activation(
                        big(ON_STG + sslot * 2048, 2048)[:, tsl], PB[tc % 2][:, :], AF.Copy),
                        inc="S")
                    s_bank[tc % 2] = p.snap("S")
                s_slab = p.snap("S")
                if _CONFIG.get("dbg4") and n == 0:
                    p.wait("sync", "S", s_slab)
                    p.run("sync", lambda e: e.dma_start(
                        dbg_t.ap()[1][:, 0:1024],
                        BIGt[:, :].bitcast(F32)[:, ON_STG // 2:ON_STG // 2 + 1024]), inc="dbx")
                    g_d2 = p.snap("dbx")
                    p.wait("sync", "dbx", g_d2)
                    p.wait("scalar", "dbx", g_d2)
                p.wait("vector", "S", s_slab)
                p.run("vector", lambda e, sslot=sslot: e.reduce_max(
                    BIS[:, 9:10], big(ON_STG + sslot * 2048, 2048), AX.X, apply_absolute_value=True))
                p.run("vector", lambda e, mxcol=mxcol: e.tensor_max(
                    MXW[:, mxcol:mxcol + 1], MXW[:, mxcol:mxcol + 1], BIS[:, 9:10]), inc="V")
                stg_v[sslot] = p.snap("V")
                p.wait("sync", "V", stg_v[sslot])
                p.run("sync", lambda e, sslot=sslot, hbm_dst=hbm_dst, ic=ic: e.dma_start(
                    hbm_dst.ap()[ic * P:(ic + 1) * P, :], big(ON_STG + sslot * 2048, 2048)), inc="st%d" % sslot)
                stg_b[sslot] = p.snap("st%d" % sslot)
        s_4a_evac = p.snap("S")
        drain_4a = [("st0", p.snap("st0")), ("st1", p.snap("st1")),
                    ("na0", p.snap("na0")), ("na1", p.snap("na1")), ("T", p.cnt["T"])]

        v_fold = p.snap("V")
        g_ug = cross_max(12, v_fold)
        g_ug = cross_max(13, g_ug)
        w_ar2 = armax(1, [12, 13], "V", g_ug)
        p.wait("vector", "cc0", w_ar2)
        p.run("vector", lambda e: e.reciprocal(MXW[:, 9:10], MXW[:, 24:25]))
        p.run("vector", lambda e: e.tensor_scalar(MXW[:, 0:1], MXW[:, 9:10], 32767.0, None, ALU.mult))
        p.run("vector", lambda e: e.tensor_scalar(MXW[:, 1:2], MXW[:, 24:25], 1.0 / 32767.0, None, ALU.mult))
        p.run("vector", lambda e: e.reciprocal(MXW[:, 9:10], MXW[:, 25:26]))
        p.run("vector", lambda e: e.tensor_scalar(MXW[:, 2:3], MXW[:, 9:10], 32767.0, None, ALU.mult))
        p.run("vector", lambda e: e.tensor_scalar(MXW[:, 3:4], MXW[:, 25:26], 1.0 / 32767.0, None, ALU.mult), inc="V")
        v_r2 = p.snap("V")

        p.scope = "p4ba"
        gb_marks = []

        def issue_gslab(n, which, ic, slot):
            h = up_hbm if which == 0 else gate_hbm
            p.run("sync", lambda e, h=h, ic=ic, slot=slot: e.dma_start(
                SQBF[:, :].bitcast(BF16)[:, slot * 2048:(slot + 1) * 2048], h.ap()[ic * P:(ic + 1) * P, :]), inc="B")
            gb_marks.append(p.snap("B"))

        p.run("vector", lambda e: e.tensor_scalar(MXW[:, 14:15], MXW[:, 14:15], 0.0, None, ALU.mult), inc="V")
        for sn, vv in drain_4a:
            p.wait("vector", sn, vv)
            p.wait("sync", sn, vv)
        v_ob = {}
        for ic in range(24):
            gsl = (ic % 2) * 2048
            p.wait("sync", "V", v_ob.get(ic - 2, 0))
            rrn = "rr%d" % (ic % 2)
            p.run("sync", lambda e, ic=ic, gsl=gsl: e.dma_start(
                SQBF[:, :].bitcast(BF16)[:, gsl:gsl + 2048], gate_hbm.ap()[ic * P:(ic + 1) * P, :]), inc=rrn)
            b_g = p.snap(rrn)
            if _CONFIG.get("dbg4") and ic == 0:
                p.wait("sync", rrn, b_g)
                p.run("sync", lambda e: e.dma_start(dbg_t.ap()[7], SQBF[:, 0:1024]), inc="dbx")
                p.wait("sync", "dbx", p.snap("dbx"))
            p.wait("vector", rrn, b_g)
            p.wait("vector", "V", v_r2)
            gslab = SQBF[:, :].bitcast(BF16)[:, gsl:gsl + 2048]
            FZ = BIGt[:, :].bitcast(F32)[:, 16384:18432]
            p.run("vector", lambda e, gslab=gslab: e.tensor_scalar(SCRA[:, :], gslab, MXW[:, 2:3], 32768.0, ALU.mult, ALU.add),
                  inc="V")
            p.wait("scalar", "V", p.snap("V"))
            p.run("scalar", lambda e: e.activation(SI32[:, :], SCRA[:, :], AF.Copy))
            p.run("scalar", lambda e: e.activation(SCRC[:, :], SI32[:, :], AF.Copy), inc="S")
            p.wait("vector", "S", p.snap("S"))
            p.run("vector", lambda e: e.scalar_tensor_tensor(
                SCRB[:, :], SCRA[:, :], 0.5, SCRC[:, :], ALU.subtract, ALU.is_ge))
            p.run("vector", lambda e: e.tensor_tensor(SCRA[:, :], SCRC[:, :], SCRB[:, :], ALU.add))
            p.run("vector", lambda e: e.tensor_scalar(SCRA[:, :], SCRA[:, :], -32768.0, MXW[:, 3:4], ALU.add, ALU.mult), inc="V")
            v_gq = p.snap("V")
            p.wait("scalar", "V", v_gq)
            p.run("scalar", lambda e: e.activation(SCRC[:, :], SCRA[:, :], AF.Sigmoid), inc="S")
            s_sig = p.snap("S")
            p.wait("vector", "S", s_sig)
            p.run("vector", lambda e: e.tensor_scalar(SCRC[:, :], SCRC[:, :], 65536.0, None, ALU.mult), inc="V")
            p.wait("scalar", "V", p.snap("V"))
            p.run("scalar", lambda e: e.activation(SI32[:, :], SCRC[:, :], AF.Copy))
            p.run("scalar", lambda e, FZ=FZ: e.activation(FZ, SI32[:, :], AF.Copy), inc="S")
            p.wait("vector", "S", p.snap("S"))
            p.run("vector", lambda e, FZ=FZ: e.scalar_tensor_tensor(
                SCRB[:, :], SCRC[:, :], 0.5, FZ, ALU.subtract, ALU.is_ge))
            p.run("vector", lambda e, FZ=FZ: e.tensor_tensor(SCRC[:, :], FZ, SCRB[:, :], ALU.add))
            p.run("vector", lambda e: e.tensor_scalar(SCRC[:, :], SCRC[:, :], 65535.0, 1.0 / 65536.0, ALU.min, ALU.mult))
            dst = ob_chunk(ic)
            p.run("vector", lambda e, dst=dst: e.tensor_mul(dst, SCRA[:, :], SCRC[:, :]))
            p.run("vector", lambda e, dst=dst: e.reduce_max(BIS[:, 9:10], dst, AX.X, apply_absolute_value=True))
            p.run("vector", lambda e: e.tensor_max(MXW[:, 14:15], MXW[:, 14:15], BIS[:, 9:10]), inc="V")
            v_ob[ic] = p.snap("V")
        if _CONFIG.get("dbg4"):
            p.wait("sync", "V", v_ob[0])
            p.run("sync", lambda e: e.dma_start(
                dbg_t.ap()[6], XLNt[:, :].bitcast(F32)[:, 0:1024]), inc="dbx")
            db_gs = p.snap("dbx")
            p.wait("sync", "dbx", db_gs)
            p.wait("vector", "dbx", db_gs)
        g_g2 = cross_max(14, v_ob[23])
        w_ar3 = armax(2, [14], "V", g_g2)
        p.wait("vector", "cc0", w_ar3)
        p.run("vector", lambda e: e.reciprocal(MXW[:, 9:10], MXW[:, 24:25]))
        p.run("vector", lambda e: e.tensor_scalar(MXW[:, 4:5], MXW[:, 9:10], 32767.0, None, ALU.mult))
        p.run("vector", lambda e: e.tensor_scalar(MXW[:, 5:6], MXW[:, 24:25], 1.0 / 32767.0, None, ALU.mult))
        p.run("vector", lambda e: e.tensor_scalar(MXW[:, 15:16], MXW[:, 15:16], 0.0, None, ALU.mult), inc="V")
        v_r3 = p.snap("V")

        p.scope = "p4bb"
        v_o = {-2: v_ob[22], -1: v_ob[23]}
        for ic in range(24):
            usl = (ic % 2) * 2048
            p.wait("sync", "V", v_o.get(ic - 2, 0))
            rrn = "rr%d" % (ic % 2)
            p.run("sync", lambda e, ic=ic, usl=usl: e.dma_start(
                SQBF[:, :].bitcast(BF16)[:, usl:usl + 2048], up_hbm.ap()[ic * P:(ic + 1) * P, :]), inc=rrn)
            b_u = p.snap(rrn)
            if _CONFIG.get("dbg4") and ic == 0:
                p.wait("sync", rrn, b_u)
                p.run("sync", lambda e: e.dma_start(dbg_t.ap()[8], SQBF[:, 0:1024]), inc="dbx")
                p.wait("sync", "dbx", p.snap("dbx"))
            p.wait("vector", rrn, b_u)
            p.wait("vector", "V", v_r3)
            uslab = SQBF[:, :].bitcast(BF16)[:, usl:usl + 2048]
            FZ = BIGt[:, :].bitcast(F32)[:, 16384:18432]
            p.run("vector", lambda e, uslab=uslab: e.tensor_scalar(SCRA[:, :], uslab, MXW[:, 0:1], 32768.0, ALU.mult, ALU.add),
                  inc="V")
            p.wait("scalar", "V", p.snap("V"))
            p.run("scalar", lambda e: e.activation(SI32[:, :], SCRA[:, :], AF.Copy))
            p.run("scalar", lambda e: e.activation(SCRC[:, :], SI32[:, :], AF.Copy), inc="S")
            p.wait("vector", "S", p.snap("S"))
            p.run("vector", lambda e: e.scalar_tensor_tensor(
                SCRB[:, :], SCRA[:, :], 0.5, SCRC[:, :], ALU.subtract, ALU.is_ge))
            p.run("vector", lambda e: e.tensor_tensor(SCRA[:, :], SCRC[:, :], SCRB[:, :], ALU.add))
            p.run("vector", lambda e: e.tensor_scalar(SCRA[:, :], SCRA[:, :], -32768.0, MXW[:, 1:2], ALU.add, ALU.mult))
            src = ob_chunk(ic)
            p.run("vector", lambda e, src=src: e.tensor_scalar(SCRC[:, :], src, MXW[:, 4:5], 32768.0, ALU.mult, ALU.add),
                  inc="V")
            p.wait("scalar", "V", p.snap("V"))
            p.run("scalar", lambda e: e.activation(SI32[:, :], SCRC[:, :], AF.Copy))
            p.run("scalar", lambda e, FZ=FZ: e.activation(FZ, SI32[:, :], AF.Copy), inc="S")
            p.wait("vector", "S", p.snap("S"))
            p.run("vector", lambda e, FZ=FZ: e.scalar_tensor_tensor(
                SCRB[:, :], SCRC[:, :], 0.5, FZ, ALU.subtract, ALU.is_ge))
            p.run("vector", lambda e, FZ=FZ: e.tensor_tensor(SCRC[:, :], FZ, SCRB[:, :], ALU.add))
            p.run("vector", lambda e: e.tensor_scalar(SCRC[:, :], SCRC[:, :], -32768.0, MXW[:, 5:6], ALU.add, ALU.mult))
            p.run("vector", lambda e, src=src: e.tensor_mul(src, SCRC[:, :], SCRA[:, :]))
            p.run("vector", lambda e, src=src: e.reduce_max(BIS[:, 9:10], src, AX.X, apply_absolute_value=True))
            p.run("vector", lambda e: e.tensor_max(MXW[:, 15:16], MXW[:, 15:16], BIS[:, 9:10]), inc="V")
            v_o[ic] = p.snap("V")
        g_o = cross_max(15, v_o[23])
        w_ar4 = armax(3, [15], "V", g_o)
        p.wait("vector", "cc0", w_ar4)
        p.run("vector", lambda e: e.reciprocal(MXW[:, 9:10], MXW[:, 24:25]))
        p.run("vector", lambda e: e.tensor_scalar(MXW[:, 6:7], MXW[:, 9:10], 32767.0, None, ALU.mult))
        p.run("vector", lambda e: e.tensor_scalar(MXW[:, 7:8], MXW[:, 24:25], 1.0 / 32767.0, None, ALU.mult), inc="V")
        v_r4 = p.snap("V")

        p.scope = "p4bc"
        v_oq = {}
        p.wait("vector", "V", v_r4)
        for ic in range(24):
            src = ob_chunk(ic)
            p.run("vector", lambda e, src=src: e.tensor_scalar(SCRA[:, :], src, MXW[:, 6:7], 32768.0, ALU.mult, ALU.add),
                  inc="V")
            p.wait("scalar", "V", p.snap("V"))
            p.run("scalar", lambda e: e.activation(SI32[:, :], SCRA[:, :], AF.Copy))
            p.run("scalar", lambda e: e.activation(SCRC[:, :], SI32[:, :], AF.Copy), inc="S")
            p.wait("vector", "S", p.snap("S"))
            p.run("vector", lambda e: e.scalar_tensor_tensor(
                SCRB[:, :], SCRA[:, :], 0.5, SCRC[:, :], ALU.subtract, ALU.is_ge))
            p.run("vector", lambda e: e.tensor_tensor(SCRA[:, :], SCRC[:, :], SCRB[:, :], ALU.add))
            p.run("vector", lambda e, src=src: e.tensor_scalar(src, SCRA[:, :], -32768.0, MXW[:, 7:8], ALU.add, ALU.mult),
                  inc="V")
            v_oq[ic] = p.snap("V")
        v_oq_all = p.snap("V")

        p.scope = "p4c"
        nat_plan2 = []
        for oc in range(NCH):
            for j in range(3):
                nat_plan2.append((oc, j))
        nat2_marks = []
        nat2_cons = []

        def issue_nat2(n):
            oc, j = nat_plan2[n]
            if n >= 2:
                p.wait("sync", "T", nat2_cons[n - 2])
            sname = "na%d" % (n % 2)
            p.run("sync", lambda e, oc=oc, j=j, n=n: e.dma_start(
                bigf(32768 + (n % 2) * 2048, 1024), wdown_n.ap()[oc * P:(oc + 1) * P, j * 1024:(j + 1) * 1024]), inc=sname)
            nat2_marks.append((sname, p.snap(sname)))

        def dequant_transpose2(n, tr_slot):
            nat = bigf(32768 + (n % 2) * 2048, 1024)
            nb = 32
            p.wait("vector", nat2_marks[n][0], nat2_marks[n][1])
            p.run("vector", lambda e, nat=nat: e.tensor_reduce(
                DQS[:, 0:32], nat.rearrange("p (b i) -> p b i", i=32), AX.X, ALU.max,
                apply_absolute_value=True))
            p.run("vector", lambda e: e.reciprocal(DQS[:, 32:64], DQS[:, 0:32]))
            p.run("vector", lambda e: e.tensor_scalar(DQS[:, 32:64], DQS[:, 32:64], 7.0, None, ALU.mult))
            p.run("vector", lambda e: e.tensor_scalar(DQS[:, 64:96], DQS[:, 0:32], 1.0 / 7.0, None, ALU.mult))
            rinv_bc = bass.AP(DQS, 32, [[96, P], [1, 32], [0, 32]])
            s_bc = bass.AP(DQS, 64, [[96, P], [1, 32], [0, 32]])
            p.run("vector", lambda e, nat=nat, rinv_bc=rinv_bc: e.tensor_mul(SCRA[:, 0:1024], nat, rinv_bc), inc="V")
            v_m = p.snap("V")
            p.wait("scalar", "V", v_m)
            p.run("scalar", lambda e: e.add(SI32[:, 0:1024], SCRA[:, 0:1024], 8.0))
            p.run("scalar", lambda e: e.activation(SCRC[:, 512:1536], SI32[:, 0:1024], AF.Copy, bias=-8.0), inc="S")
            s_rt = p.snap("S")
            p.wait("vector", "S", s_rt)
            p.run("vector", lambda e: e.scalar_tensor_tensor(
                SCRB[:, 0:1024], SCRA[:, 0:1024], 0.5, SCRC[:, 512:1536], ALU.subtract, ALU.is_ge))
            p.run("vector", lambda e: e.tensor_tensor(
                SCRC[:, 512:1536], SCRC[:, 512:1536], SCRB[:, 0:1024], ALU.add))
            p.run("vector", lambda e, s_bc=s_bc: e.tensor_mul(
                SCRC[:, 0:512].bitcast(BF16)[:, 0:1024], SCRC[:, 512:1536], s_bc), inc="V")
            v_dq = p.snap("V")
            deq = SCRC[:, 0:512].bitcast(BF16)[:, 0:1024]
            for j in range(NCH):
                p.wait("tensor", "V", v_dq)
                p.wait("tensor", "V", state["pt_free_v"])
                p.run("tensor", lambda e, deq=deq, j=j: e.transpose(
                    PT[:, 0:128], deq[:, j * P:(j + 1) * P], IDENT[:, :]), inc="T")
                t_tr = p.cnt["T"]
                p.wait("vector", "T", t_tr)
                p.run("vector", lambda e, j=j, tr_slot=tr_slot: e.tensor_copy(
                    big(36864 + tr_slot * 1024 + j * P, P), PT[:, 0:128]), inc="V")
                state["pt_free_v"] = p.snap("V")
            return p.snap("V")

        p.wait("sync", "V", v_o[23])
        issue_nat2(0)
        issue_nat2(1)
        v_dn = [0, 0, 0, 0]
        p.wait("tensor", "S", s_4a_evac)
        for oc in range(NCH):
            for j in range(3):
                n = oc * 3 + j
                while len(nat2_marks) <= min(n + 1, 23):
                    issue_nat2(len(nat2_marks))
                v_tr = dequant_transpose2(n, n % 2)
                p.wait("tensor", "V", v_tr)
                p.wait("tensor", "V", v_oq_all)
                for tc in range(4):
                    tsl = slice(tc * 512, (tc + 1) * 512)
                    if j == 0:
                        p.wait("tensor", "V", v_dn[tc])
                    for kci in range(NCH):
                        kc = 8 * j + kci
                        p.run("tensor", lambda e, kc=kc, kci=kci, tsl=tsl, tc=tc, j=j, n=n: e.matmul(
                            PB[tc][:, :], big(36864 + (n % 2) * 1024 + kci * P, P),
                            ob_chunk(kc)[:, tsl],
                            start=(j == 0 and kci == 0), stop=(j == 2 and kci == NCH - 1)),
                            inc="T" if kci == NCH - 1 else None)
                    if tc == 3:
                        nat2_cons.append(p.cnt["T"])
                    if j == 2:
                        t_dn = p.cnt["T"]
                        p.wait("vector", "T", t_dn)
                        dst = RX[:, oc * T:(oc + 1) * T][:, tsl]
                        p.run("vector", lambda e, dst=dst, tc=tc: e.tensor_add(dst, dst, PB[tc][:, :]), inc="V")
                        v_dn[tc] = p.snap("V")
        if _CONFIG.get("dbg4"):
            p.wait("sync", "V", p.cnt["V"])
            p.run("sync", lambda e: e.dma_start(dbg_t.ap()[5][:, 0:32], MXW[:, :]), inc="dbx")
            p.wait("sync", "dbx", p.snap("dbx"))
            p.run("sync", lambda e: e.dma_start(dbg_t.ap()[5][:, 32:48], BIS[:, :]), inc="dbx")
            p.wait("sync", "dbx", p.snap("dbx"))
            for k in range(6):
                src = XLNt[:, :].bitcast(F32)[:, k * 1024:(k + 1) * 1024]
                p.run("sync", lambda e, k=k, src=src: e.dma_start(dbg_t.ap()[10 + k], src), inc="dbx")
                p.wait("sync", "dbx", p.snap("dbx"))
        return p.snap("V")

    def store_out(prev_v):
        for _eng in ("gpsimd", "sync", "scalar"):
            p.wait(_eng, "T", p.cnt["T"])
            p.wait(_eng, "V", p.cnt["V"])
            p.wait(_eng, "S", p.cnt["S"])
        p.wait("tensor", "V", prev_v)
        vq = 0
        b_marks = {}
        for it in range(16):
            slot = (it % 2) * 1024
            for c in range(NCH):
                p.wait("tensor", "V", state["pt_free_v"])
                if c == 0:
                    p.wait("vector", "st%d" % (it % 2), b_marks.get(it - 2, 0))
                p.run("tensor", lambda e, it=it, c=c: e.transpose(
                    PXB[:, 0:128], RX[:, c * T:(c + 1) * T][:, it * P:(it + 1) * P], IDENTF[:, :]), inc="T")
                t_tr = p.cnt["T"]
                p.wait("vector", "T", t_tr)
                p.run("vector", lambda e, it=it, c=c, slot=slot: e.tensor_copy(
                    SCRB[:, slot + c * P:slot + (c + 1) * P], PXB[:, 0:128]), inc="V")
                state["pt_free_v"] = p.snap("V")
            v_row = p.snap("V")
            p.wait("sync", "V", v_row)
            p.run("sync", lambda e, it=it, slot=slot: e.dma_start(
                out_t.ap()[it * P:(it + 1) * P, :], SCRB[:, slot:slot + 1024]), inc="st%d" % (it % 2))
            b_marks[it] = p.snap("st%d" % (it % 2))
        for sname in ("B", "W", "Q", "st0", "st1", "gb0", "cc0", "rr0", "rr1", "q0", "q1",
                      "wv0", "wv1", "na0", "na1") + tuple("R%d" % i for i in range(8)):
            p.wait("sync", sname, p.snap(sname))

    p.scope = "p1"
    v1 = phase1()
    if phases >= 2:
        p.scope = "p2"
        v1 = phase2(v1)
    if phases >= 3:
        p.scope = "p3"
        v1 = phase3(v1)
    if phases >= 4:
        p.scope = "p4"
        v1 = phase4_real(v1)
    p.scope = "st"
    store_out(v1)

    p.finalize(nc)
    for cm in reversed(cms):
        cm.__exit__(None, None, None)
    return nc


def _host_constants():
    ident = np.eye(P, dtype=ml_dtypes.bfloat16)
    diagm = np.triu(np.ones((P, P), dtype=np.float32)).astype(ml_dtypes.bfloat16)
    ish = np.zeros((64, P), dtype=ml_dtypes.bfloat16)
    for k in range(64):
        ish[k, k + 64] = 1.0
    oh = np.zeros((L1_NB, 1280), dtype=np.float32)
    maskadd = np.full((1, 1280), NEG_BIG, dtype=np.float32)
    for idx in range(1280):
        n = idx - 511
        if 0 <= n <= 511:
            oh[_bucket_of(n), idx] = 1.0
            maskadd[0, idx] = 0.0
    consts = np.zeros(32, dtype=np.float32)
    consts[:len(CONST_VALS)] = np.asarray(CONST_VALS, dtype=np.float32)
    return ident, diagm, ish, oh, maskadd, consts


def make_in_maps(L0_attn_x, L0_attn_Wqkv, L0_attn_bqkv, L0_attn_Wout, L0_attn_bout,
                 L0_ffn_gate_weight, L0_ffn_w1, L0_ffn_b1, L0_ffn_w2, L0_ffn_b2,
                 L1_attn_Wqk, L1_attn_bqk, L1_attn_Wv, L1_attn_bv, L1_attn_gamma,
                 L1_attn_beta, L1_attn_rel_emb, L1_ffn_w_gate, L1_ffn_w_up, L1_ffn_w_down):
    x = np.asarray(L0_attn_x)
    ident, diagm, ish, oh, maskadd, consts = _host_constants()
    shared = dict(
        wqkv=np.ascontiguousarray(np.asarray(L0_attn_Wqkv), dtype=np.float32),
        bqkv_r=np.ascontiguousarray(np.asarray(L0_attn_bqkv).reshape(24, P).T),
        wout=np.ascontiguousarray(np.asarray(L0_attn_Wout), dtype=np.float32),
        bout_r=np.ascontiguousarray(np.asarray(L0_attn_bout).reshape(8, P).T),
        gwT=np.ascontiguousarray(np.asarray(L0_ffn_gate_weight).T),
        w1T=np.ascontiguousarray(np.asarray(L0_ffn_w1).transpose(0, 2, 1)),
        w2T=np.ascontiguousarray(np.asarray(L0_ffn_w2).transpose(0, 2, 1)),
        wqk=np.ascontiguousarray(np.asarray(L1_attn_Wqk)),
        wv=np.ascontiguousarray(np.asarray(L1_attn_Wv)),
        gammaT=np.ascontiguousarray(np.asarray(L1_attn_gamma).T),
        rel_in=np.ascontiguousarray(np.asarray(L1_attn_rel_emb)),
        wup_n=np.ascontiguousarray(np.asarray(L1_ffn_w_up)),
        wgate_n=np.ascontiguousarray(np.asarray(L1_ffn_w_gate)),
        wdown_n=np.ascontiguousarray(np.asarray(L1_ffn_w_down)),
        ident_c=ident, diagm_c=diagm, ishift_c=ish, oh_c=oh, maskadd_c=maskadd,
        consts_c=consts,
    )
    in_maps = []
    for c in range(8):
        im = dict(shared)
        im["xT"] = np.ascontiguousarray(x[c].T)
        in_maps.append(im)
    return in_maps


def kernel(**inputs):
    nc = build_nc(_CONFIG["phases"])
    in_maps = make_in_maps(**inputs)
    res = run_bass_kernel_spmd(nc, in_maps, core_ids=list(range(8)))
    out = np.stack([res.results[c]["out"] for c in range(8)], axis=0)
    return out
